# version 07
# baseline (speedup 1.0000x reference)
import numpy as np
import ml_dtypes
import concourse.bass as bass
import concourse.mybir as mybir
from concourse.bass_utils import run_bass_kernel_spmd

F32 = mybir.dt.float32
BF16 = mybir.dt.bfloat16
I32 = mybir.dt.int32
AF = mybir.ActivationFunctionType
OP = mybir.AluOpType

B, L, D = 4, 4096, 2048
CTOK = 2048
LN_EPS = 1e-5


class AutoDrain:
    _OPS = {
        'tensor_scalar', 'scalar_tensor_tensor', 'tensor_tensor', 'tensor_copy',
        'tensor_add', 'tensor_mul', 'tensor_sub', 'tensor_max', 'bn_stats',
        'bn_aggr', 'reciprocal', 'memset', 'tensor_reduce', 'reduce_max',
        'reduce_sum', 'activation', 'select', 'copy_predicated', 'iota',
        'mul', 'add', 'copy', 'max', 'transpose', 'sign', 'sqrt', 'square',
        'memzero', 'max_index', 'tensor_relu', 'tensor_scalar_add',
        'tensor_scalar_mul', 'tensor_scalar_max', 'tensor_scalar_min',
    }

    def __init__(self, eng):
        object.__setattr__(self, '_e', eng)

    def __getattr__(self, name):
        f = getattr(self._e, name)
        if name in AutoDrain._OPS:
            def wrapped(*a, **k):
                r = f(*a, **k)
                self._e.drain()
                return r
            return wrapped
        return f



def build_l0a_heads(Lp=L):
    nc = bass.Bass("TRN2", target_bir_lowering=False)
    xb = nc.dram_tensor("xb", [Lp, D], F32, kind="ExternalInput").ap()
    Wqk = nc.dram_tensor("Wqk", [D, 1024], F32, kind="ExternalInput").ap()
    bqk_d = nc.dram_tensor("bqk", [1, 1024], F32, kind="ExternalInput").ap()
    Wv = nc.dram_tensor("Wv", [D, 1024], F32, kind="ExternalInput").ap()
    bv_d = nc.dram_tensor("bv", [1, 1024], F32, kind="ExternalInput").ap()
    out = nc.dram_tensor("attn_out", [Lp, 1024], F32, kind="ExternalOutput").ap()
    ln0d = nc.dram_tensor("ln0d", [Lp, D], BF16)
    phid = nc.dram_tensor("phid", [Lp, 1024], BF16)

    NT = Lp // 128
    NB = Lp // 512
    NKC = D // 128

    from contextlib import ExitStack
    st = ExitStack()
    def sb(*a, **k):
        return st.enter_context(nc.sbuf_tensor(*a, **k))
    def ps(*a, **k):
        return st.enter_context(nc.psum_tensor(*a, **k))
    def sem(n, k=1):
        if k == 1:
            return st.enter_context(nc.semaphore(n))
        return [st.enter_context(nc.semaphore(f"{n}{i}")) for i in range(k)]

    xt = sb("xt", [128, 4 * D], F32)
    bst = sb("bst", [128, 24], F32)
    mv = sb("mv", [128, 2], F32)
    nmu_a = sb("nmu_a", [128, NT], F32)
    veps_a = sb("veps_a", [128, NT], F32)
    std_a = sb("std_a", [128, NT], F32)
    rstd_a = sb("rstd_a", [128, NT], F32)
    ln0t = sb("ln0t", [128, 2 * D], BF16)
    lnT = sb("lnT", [128, 2 * NKC * 512], BF16, is_transpose_dest=True)
    wqk = sb("wqk", [128, NKC * 1024], BF16)
    wv = sb("wv", [128, NKC * 1024], BF16)
    bqk_s = sb("bqk_s", [1, 1024], BF16)
    bv_s = sb("bv_s", [1, 1024], BF16)
    onesr = sb("onesr", [1, 128], BF16)
    zmin = sb("zmin", [128, 1024], BF16)
    zmax = sb("zmax", [128, 1024], BF16)
    zexp = sb("zexp", [128, 1024], BF16)
    vtan = sb("vtan", [128, 1024], BF16)
    phi = sb("phi", [128, 2 * 1024], BF16)
    vsil = sb("vsil", [128, 2 * 1024], BF16)
    kvsb = sb("kvsb", [128, 1024], BF16)
    phiT = sb("phiT", [128, 2 * 512], BF16, is_transpose_dest=True)
    osb = sb("osb", [128, 2 * 4 * 1024], F32)
    pz = ps("pz", [128, 1024], F32)
    pv = ps("pv", [128, 1024], F32)
    pkv = ps("pkv", [128, 1024], F32)
    po = ps("po", [128, 1024], F32)

    s_w = sem("s_w")
    s_wh = sem("s_wh")
    s_x1 = sem("s_x1", 2)
    s_st1 = sem("s_stat")
    s_sq = sem("s_sq")
    s_x2 = sem("s_x2", 2)
    s_ln = sem("s_ln")
    s_lnd = sem("s_lnd", 2)
    s_lnT = sem("s_lnT", 2)
    s_proj = sem("s_proj")
    s_zmm = sem("s_zmm")
    s_actz = sem("s_actz")
    s_phiv = sem("s_phiv")
    s_phid = sem("s_phid", 2)
    s_kvacc = sem("s_kvacc")
    s_kvev = sem("s_kvev")
    s_phiT = sem("s_phiT", 2)
    s_oe = sem("s_oe")
    s_oev = sem("s_oev")
    s_st = sem("s_st", 2)
    block = st.enter_context(nc.Block())
    if True:
        @block.sync
        def _(e):
            for i in range(NT):
                if i >= 2:
                    e.wait_ge(s_st1, i - 1)
                e.dma_start(out=xt[:, (i % 2) * D:(i % 2 + 1) * D],
                            in_=xb[i * 128:(i + 1) * 128, :]).then_inc(s_x1[i % 2], 16)
            for i in range(NT):
                if i >= 2:
                    e.wait_ge(s_ln, i - 1)
                e.dma_start(out=xt[:, (2 + i % 2) * D:(2 + i % 2 + 1) * D],
                            in_=xb[i * 128:(i + 1) * 128, :]).then_inc(s_x2[i % 2], 16)
            for tb in range(NB):
                e.wait_ge(s_oev, 8 * (tb + 1))
                for sub in range(4):
                    e.dma_start(out=out[(4 * tb + sub) * 128:(4 * tb + sub + 1) * 128, :],
                                in_=osb[:, (tb % 2) * 4096 + sub * 1024:(tb % 2) * 4096 + (sub + 1) * 1024]
                                ).then_inc(s_st[tb % 2], 16)
            e.wait_ge(s_st[0], 16 * 4 * ((NB + 1) // 2))
            e.wait_ge(s_st[1], 16 * 4 * (NB // 2))

        @block.gpsimd
        def _(e):
            for c in range(NKC):
                e.dma_start(out=wqk[:, c * 1024:(c + 1) * 1024],
                            in_=Wqk[c * 128:(c + 1) * 128, :]).then_inc(s_w, 16)
                e.dma_start(out=wv[:, c * 1024:(c + 1) * 1024],
                            in_=Wv[c * 128:(c + 1) * 128, :]).then_inc(s_w, 16)
            e.dma_start(out=bqk_s[:], in_=bqk_d[:, :]).then_inc(s_w, 16)
            e.dma_start(out=bv_s[:], in_=bv_d[:, :]).then_inc(s_w, 16)
            for i in range(NT):
                e.wait_ge(s_ln, i + 1)
                e.dma_start(out=ln0d.ap()[i * 128:(i + 1) * 128, :],
                            in_=ln0t[:, (i % 2) * D:(i % 2 + 1) * D]).then_inc(s_lnd[i % 2], 16)
            for s in range(NT):
                e.wait_ge(s_phiv, s + 1)
                e.dma_start(out=phid.ap()[s * 128:(s + 1) * 128, :],
                            in_=phi[:, (s % 2) * 1024:(s % 2 + 1) * 1024]).then_inc(s_phid[s % 2], 16)

        @block.vector
        def _(e):
            e = AutoDrain(e)
            e.memset(onesr[:], 1.0)
            e.wait_ge(s_w, 16 * (2 * NKC + 2))
            e.tensor_scalar(wv[:], wv[:], 0.5, None, OP.mult)
            e.tensor_scalar(bv_s[:], bv_s[:], 0.5, None, OP.mult).then_inc(s_wh, 1)
            for i in range(NT):
                e.wait_ge(s_x1[i % 2], 16 * (i // 2 + 1))
                xs = xt[:, (i % 2) * D:(i % 2 + 1) * D]
                for c in range(4):
                    e.bn_stats(bst[:, 6 * c:6 * (c + 1)], xs[:, 512 * c:512 * (c + 1)])
                e.bn_aggr(mv[:], bst[:])
                e.tensor_scalar(nmu_a[:, i:i + 1], mv[:, 0:1], -1.0, None, OP.mult)
                e.tensor_scalar(veps_a[:, i:i + 1], mv[:, 1:2], LN_EPS, None,
                                OP.add).then_inc(s_st1, 1)
            e.wait_ge(s_sq, 1)
            e.reciprocal(rstd_a[:], std_a[:])
            for i in range(NT):
                e.wait_ge(s_x2[i % 2], 16 * (i // 2 + 1))
                if i >= 2:
                    e.wait_ge(s_lnd[i % 2], 16 * (i // 2))
                xs = xt[:, (2 + i % 2) * D:(2 + i % 2 + 1) * D]
                e.tensor_scalar(ln0t[:, (i % 2) * D:(i % 2 + 1) * D], xs,
                                nmu_a[:, i:i + 1], rstd_a[:, i:i + 1],
                                OP.add, OP.mult).then_inc(s_ln, 1)
            for s in range(NT):
                e.wait_ge(s_proj, s + 1)
                if s >= 2:
                    e.wait_ge(s_kvacc, s - 1)
                    e.wait_ge(s_phid[s % 2], 16 * (s // 2))
                e.tensor_scalar(zmin[:], pz[:], 0.0, None, OP.min)
                e.tensor_scalar(zmax[:], pz[:], 0.0, None, OP.max).then_inc(s_zmm, 1)
                e.wait_ge(s_actz, s + 1)
                e.scalar_tensor_tensor(phi[:, (s % 2) * 1024:(s % 2 + 1) * 1024],
                                       zexp[:], -1.0, zmax[:], OP.add, OP.add)
                e.scalar_tensor_tensor(vsil[:, (s % 2) * 1024:(s % 2 + 1) * 1024],
                                       vtan[:], 1.0, pv[:], OP.add, OP.mult).then_inc(s_phiv, 1)
            e.wait_ge(s_kvacc, NT)
            e.tensor_copy(kvsb[:, :512], pkv[:, :512])
            e.tensor_copy(kvsb[:, 512:], pkv[:, 512:]).then_inc(s_kvev, 1)
            for tb in range(NB):
                if tb >= 2:
                    e.wait_ge(s_st[tb % 2], 16 * 4 * (tb // 2))
                for h in range(8):
                    idx = tb * 8 + h
                    e.wait_ge(s_oe, idx + 1)
                    for sub in range(4):
                        dst = osb[:, (tb % 2) * 4096 + sub * 1024 + h * 128:
                                  (tb % 2) * 4096 + sub * 1024 + (h + 1) * 128]
                        cp = e.tensor_copy(dst, po[:, (idx % 2) * 512 + sub * 128:
                                                   (idx % 2) * 512 + (sub + 1) * 128])
                        if sub == 3:
                            cp.then_inc(s_oev, 1)

        @block.scalar
        def _(e):
            e.wait_ge(s_st1, NT)
            e.activation(std_a[:], veps_a[:], AF.Sqrt).then_inc(s_sq, 1)
            for j in range(NB):
                e.wait_ge(s_lnd[0], 16 * (4 * (j + 1) // 2))
                e.wait_ge(s_lnd[1], 16 * (4 * (j + 1) // 2))
                if j >= 2:
                    e.wait_ge(s_proj, 4 * (j - 1))
                for c in range(NKC):
                    e.dma_start(out=lnT[:, (j % 2) * 8192 + c * 512:(j % 2) * 8192 + (c + 1) * 512],
                                in_=ln0d.ap()[j * 512:(j + 1) * 512, c * 128:(c + 1) * 128],
                                transpose=True).then_inc(s_lnT[j % 2], 16)
                if j >= 1:
                    for sub in range(4):
                        s = (j - 1) * 4 + sub
                        e.wait_ge(s_zmm, s + 1)
                        e.activation(zexp[:], zmin[:], AF.Exp)
                        e.activation(vtan[:], pv[:], AF.Tanh).then_inc(s_actz, 1)
            for sub in range(4):
                s = (NB - 1) * 4 + sub
                e.wait_ge(s_zmm, s + 1)
                e.activation(zexp[:], zmin[:], AF.Exp)
                e.activation(vtan[:], pv[:], AF.Tanh).then_inc(s_actz, 1)
            for tb in range(NB):
                e.wait_ge(s_phid[0], 16 * (2 * tb + 2))
                e.wait_ge(s_phid[1], 16 * (2 * tb + 2))
                for h in range(8):
                    idx = tb * 8 + h
                    if idx >= 2:
                        e.wait_ge(s_oe, idx - 1)
                    e.dma_start(out=phiT[:, (idx % 2) * 512:(idx % 2 + 1) * 512],
                                in_=phid.ap()[tb * 512:(tb + 1) * 512, h * 128:(h + 1) * 128],
                                transpose=True).then_inc(s_phiT[idx % 2], 16)

        @block.tensor
        def _(e):
            e.wait_ge(s_w, 16 * (2 * NKC + 2))
            e.wait_ge(s_wh, 1)
            for s in range(NT):
                j = s // 4
                e.wait_ge(s_lnT[j % 2], 256 * (j // 2 + 1))
                if s >= 1:
                    e.wait_ge(s_zmm, s)
                    e.wait_ge(s_phiv, s)
                base = (j % 2) * 8192
                sub = s % 4
                for half in range(2):
                    p = pz[:, half * 512:(half + 1) * 512]
                    for c in range(NKC):
                        e.matmul(p, lnT[:, base + c * 512 + sub * 128:base + c * 512 + (sub + 1) * 128],
                                 wqk[:, c * 1024 + half * 512:c * 1024 + (half + 1) * 512],
                                 start=(c == 0), stop=False)
                    e.matmul(p, onesr[:], bqk_s[:, half * 512:(half + 1) * 512],
                             start=False, stop=True)
                for half in range(2):
                    p = pv[:, half * 512:(half + 1) * 512]
                    for c in range(NKC):
                        e.matmul(p, lnT[:, base + c * 512 + sub * 128:base + c * 512 + (sub + 1) * 128],
                                 wv[:, c * 1024 + half * 512:c * 1024 + (half + 1) * 512],
                                 start=(c == 0), stop=False)
                    mm = e.matmul(p, onesr[:], bv_s[:, half * 512:(half + 1) * 512],
                                  start=False, stop=True)
                    if half == 1:
                        mm.then_inc(s_proj, 1)
                if s >= 1:
                    sp = s - 1
                    e.wait_ge(s_phiv, sp + 1)
                    for h in range(8):
                        mm = e.matmul(pkv[:, h * 128:(h + 1) * 128],
                                      phi[:, (sp % 2) * 1024 + h * 128:(sp % 2) * 1024 + (h + 1) * 128],
                                      vsil[:, (sp % 2) * 1024 + h * 128:(sp % 2) * 1024 + (h + 1) * 128],
                                      start=(sp == 0 and h % 4 == 0), stop=False)
                        if h == 7:
                            mm.then_inc(s_kvacc, 1)
            sp = NT - 1
            e.wait_ge(s_phiv, sp + 1)
            for h in range(8):
                mm = e.matmul(pkv[:, h * 128:(h + 1) * 128],
                              phi[:, (sp % 2) * 1024 + h * 128:(sp % 2) * 1024 + (h + 1) * 128],
                              vsil[:, (sp % 2) * 1024 + h * 128:(sp % 2) * 1024 + (h + 1) * 128],
                              start=False, stop=(h % 4 == 3))
                if h == 7:
                    mm.then_inc(s_kvacc, 1)
            e.wait_ge(s_kvev, 1)
            for tb in range(NB):
                for h in range(8):
                    idx = tb * 8 + h
                    e.wait_ge(s_phiT[idx % 2], 16 * (idx // 2 + 1))
                    if idx >= 2:
                        e.wait_ge(s_oev, idx - 1)
                    for sub in range(4):
                        mm = e.matmul(po[:, (idx % 2) * 512 + sub * 128:(idx % 2) * 512 + (sub + 1) * 128],
                                      phiT[:, (idx % 2) * 512 + sub * 128:(idx % 2) * 512 + (sub + 1) * 128],
                                      kvsb[:, h * 128:(h + 1) * 128],
                                      start=(sub == 0), stop=(sub == 3))
                        if sub == 3:
                            mm.then_inc(s_oe, 1)
    st.close()
    return nc



def emit_rsqrt(e, out_f32, in_f32, tmp_f32):
    ai = in_f32.bitcast(I32)
    hi = out_f32.bitcast(I32)
    e.tensor_scalar(hi, ai, 1, None, OP.logical_shift_right)
    e.tensor_scalar(hi, hi, -1, 0x5f3759df, OP.mult, OP.add)
    for _ in range(3):
        e.tensor_tensor(tmp_f32, out_f32, out_f32, OP.mult)
        e.tensor_tensor(tmp_f32, tmp_f32, in_f32, OP.mult)
        e.tensor_scalar(tmp_f32, tmp_f32, -0.5, 1.5, OP.mult, OP.add)
        e.tensor_tensor(out_f32, out_f32, tmp_f32, OP.mult)


def emit_ln_stats(e, bst, mv, nmu, veps, rstd, tmp1, xs, eps=LN_EPS, nchunks=4):
    for c in range(nchunks):
        e.bn_stats(bst[:, 6 * c:6 * (c + 1)], xs[:, 512 * c:512 * (c + 1)])
    e.bn_aggr(mv[:, :2], bst[:, :6 * nchunks])
    e.tensor_scalar(nmu[:], mv[:, 0:1], -1.0, None, OP.mult)
    e.tensor_scalar(veps[:], mv[:, 1:2], eps, None, OP.add)
    emit_rsqrt(e, rstd[:], veps[:], tmp1[:])



def build_l0a_epi(Tp=CTOK):
    nc = bass.Bass("TRN2", target_bir_lowering=False)
    xo = nc.dram_tensor("xo", [Tp, D], F32, kind="ExternalInput").ap()
    ao = nc.dram_tensor("ao", [Tp, D], F32, kind="ExternalInput").ap()
    Wu_d = nc.dram_tensor("Wu", [D, D], F32, kind="ExternalInput").ap()
    bu_d = nc.dram_tensor("bu", [1, D], F32, kind="ExternalInput").ap()
    Wo_d = nc.dram_tensor("Wo", [D, D], F32, kind="ExternalInput").ap()
    bo_d = nc.dram_tensor("bo", [1, D], F32, kind="ExternalInput").ap()
    gmap_d = nc.dram_tensor("gmap", [128, D], F32, kind="ExternalInput").ap()
    bmap_d = nc.dram_tensor("bmap", [128, D], F32, kind="ExternalInput").ap()
    x1 = nc.dram_tensor("x1", [Tp, D], F32, kind="ExternalOutput").ap()
    ln0d = nc.dram_tensor("ln0d", [Tp, D], BF16)
    gatd = nc.dram_tensor("gatd", [Tp, D], BF16)

    NT = Tp // 128
    NB = Tp // 512
    NKC = D // 128

    from contextlib import ExitStack
    st = ExitStack()
    def sb(*a, **k):
        return st.enter_context(nc.sbuf_tensor(*a, **k))
    def ps(*a, **k):
        return st.enter_context(nc.psum_tensor(*a, **k))
    def sem(n, k=1):
        if k == 1:
            return st.enter_context(nc.semaphore(n))
        return [st.enter_context(nc.semaphore(f"{n}{i}")) for i in range(k)]

    xt = sb("xt", [128, 2 * D], F32)
    at = sb("at", [128, 2 * D], F32)
    bst = sb("bst", [128, 24], F32)
    mv = sb("mv", [128, 2], F32)
    nmu = sb("nmu", [128, 1], F32)
    veps = sb("veps", [128, 1], F32)
    rstd = sb("rstd", [128, 1], F32)
    tmp1 = sb("tmp1", [128, 1], F32)
    ln0t = sb("ln0t", [128, 2 * D], BF16)
    lnT = sb("lnT", [128, 2 * NKC * 512], BF16, is_transpose_dest=True)
    wB = sb("wB", [128, NKC * D], BF16)
    bu_s = sb("bu_s", [1, D], BF16)
    bo_s = sb("bo_s", [1, D], BF16)
    gmap = sb("gmap_s", [128, D], BF16)
    bmap = sb("bmap_s", [128, D], BF16)
    onesr = sb("onesr", [1, 128], BF16)
    vtan = sb("vtan", [128, D], BF16)
    usil = sb("usil", [128, D], F32)
    laff = sb("laff", [128, D], F32)
    gat = sb("gat", [128, 2 * D], BF16)
    res = sb("res", [128, 2 * D], F32)
    pu = ps("pu", [128, 2048], F32)
    pw = ps("pw", [128, 2048], F32)

    s_w = sem("s_w")
    s_w2 = sem("s_w2")
    s_wh = sem("s_wh")
    s_xo = sem("s_xo", 2)
    s_xo2 = sem("s_xo2", 2)
    s_ao = sem("s_ao", 2)
    s_ln = sem("s_ln")
    s_lnd = sem("s_lnd", 2)
    s_lnT = sem("s_lnT", 2)
    s_up = sem("s_up")
    s_ut = sem("s_ut")
    s_gat = sem("s_gat")
    s_gd = sem("s_gd", 2)
    s_gT = sem("s_gT", 2)
    s_wo = sem("s_wo")
    s_res = sem("s_res")
    s_st = sem("s_st", 2)
    block = st.enter_context(nc.Block())
    if True:
        @block.sync
        def _(e):
            for i in range(NT):
                if i >= 2:
                    e.wait_ge(s_ln, i - 1)
                e.dma_start(out=xt[:, (i % 2) * D:(i % 2 + 1) * D],
                            in_=xo[i * 128:(i + 1) * 128, :]).then_inc(s_xo[i % 2], 16)
            for i in range(NT):
                if i >= 2:
                    e.wait_ge(s_gat, i - 1)
                e.dma_start(out=at[:, (i % 2) * D:(i % 2 + 1) * D],
                            in_=ao[i * 128:(i + 1) * 128, :]).then_inc(s_ao[i % 2], 16)
            e.wait_ge(s_gat, NT)
            for i in range(NT + 2):
                if i < NT:
                    if i >= 2:
                        e.wait_ge(s_res, i - 1)
                    e.dma_start(out=at[:, (i % 2) * D:(i % 2 + 1) * D],
                                in_=xo[i * 128:(i + 1) * 128, :]).then_inc(s_xo2[i % 2], 16)
                if i >= 2:
                    k = i - 2
                    e.wait_ge(s_res, k + 1)
                    e.dma_start(out=x1[k * 128:(k + 1) * 128, :],
                                in_=res[:, (k % 2) * D:(k % 2 + 1) * D]).then_inc(s_st[k % 2], 16)
            e.wait_ge(s_st[0], 16 * ((NT + 1) // 2))
            e.wait_ge(s_st[1], 16 * (NT // 2))

        @block.gpsimd
        def _(e):
            for c in range(NKC):
                e.dma_start(out=wB[:, c * D:(c + 1) * D],
                            in_=Wu_d[c * 128:(c + 1) * 128, :]).then_inc(s_w, 16)
            e.dma_start(out=bu_s[:], in_=bu_d[:, :]).then_inc(s_w, 16)
            e.dma_start(out=bo_s[:], in_=bo_d[:, :]).then_inc(s_w, 16)
            e.dma_start(out=gmap[:], in_=gmap_d[:, :]).then_inc(s_w, 16)
            e.dma_start(out=bmap[:], in_=bmap_d[:, :]).then_inc(s_w, 16)
            for i in range(NT):
                e.wait_ge(s_ln, i + 1)
                e.dma_start(out=ln0d.ap()[i * 128:(i + 1) * 128, :],
                            in_=ln0t[:, (i % 2) * D:(i % 2 + 1) * D]).then_inc(s_lnd[i % 2], 16)
            for i in range(NT):
                e.wait_ge(s_gat, i + 1)
                e.dma_start(out=gatd.ap()[i * 128:(i + 1) * 128, :],
                            in_=gat[:, (i % 2) * D:(i % 2 + 1) * D]).then_inc(s_gd[i % 2], 16)
            e.wait_ge(s_up, NT)
            for c in range(NKC):
                e.dma_start(out=wB[:, c * D:(c + 1) * D],
                            in_=Wo_d[c * 128:(c + 1) * 128, :]).then_inc(s_w2, 16)

        @block.vector
        def _(e):
            e = AutoDrain(e)
            e.memset(onesr[:], 1.0)
            e.wait_ge(s_w, 16 * (NKC + 4))
            e.tensor_scalar(wB[:], wB[:], 0.5, None, OP.mult)
            e.tensor_scalar(bu_s[:], bu_s[:], 0.5, None, OP.mult).then_inc(s_wh, 1)
            for i in range(NT):
                e.wait_ge(s_xo[i % 2], 16 * (i // 2 + 1))
                if i >= 2:
                    e.wait_ge(s_lnd[i % 2], 16 * (i // 2))
                xs = xt[:, (i % 2) * D:(i % 2 + 1) * D]
                emit_ln_stats(e, bst, mv, nmu, veps, rstd, tmp1, xs)
                e.tensor_scalar(ln0t[:, (i % 2) * D:(i % 2 + 1) * D], xs,
                                nmu[:], rstd[:], OP.add, OP.mult).then_inc(s_ln, 1)
            for i in range(NT):
                e.wait_ge(s_ut, i + 1)
                e.scalar_tensor_tensor(usil[:], vtan[:], 1.0, pu[:], OP.add, OP.mult)
                e.wait_ge(s_ao[i % 2], 16 * (i // 2 + 1))
                asl = at[:, (i % 2) * D:(i % 2 + 1) * D]
                emit_ln_stats(e, bst, mv, nmu, veps, rstd, tmp1, asl)
                e.tensor_scalar(laff[:], asl, nmu[:], rstd[:], OP.add, OP.mult)
                e.scalar_tensor_tensor(laff[:], laff[:], 1.0, gmap[:], OP.mult, OP.mult)
                e.tensor_tensor(laff[:], laff[:], bmap[:], OP.add)
                if i >= 2:
                    e.wait_ge(s_gd[i % 2], 16 * (i // 2))
                e.tensor_tensor(gat[:, (i % 2) * D:(i % 2 + 1) * D], usil[:], laff[:],
                                OP.mult).then_inc(s_gat, 1)
            for i in range(NT):
                e.wait_ge(s_wo, i + 1)
                e.wait_ge(s_xo2[i % 2], 16 * (i // 2 + 1))
                if i >= 2:
                    e.wait_ge(s_st[i % 2], 16 * (i // 2))
                emit_ln_stats(e, bst, mv, nmu, veps, rstd, tmp1, pw[:])
                e.tensor_scalar(res[:, (i % 2) * D:(i % 2 + 1) * D], pw[:],
                                nmu[:], rstd[:], OP.add, OP.mult)
                e.tensor_tensor(res[:, (i % 2) * D:(i % 2 + 1) * D],
                                res[:, (i % 2) * D:(i % 2 + 1) * D],
                                at[:, (i % 2) * D:(i % 2 + 1) * D],
                                OP.add).then_inc(s_res, 1)

        @block.scalar
        def _(e):
            e = AutoDrain(e)
            for j in range(NB):
                e.wait_ge(s_lnd[0], 16 * ((4 * (j + 1) + 1) // 2))
                e.wait_ge(s_lnd[1], 16 * (4 * (j + 1) // 2))
                if j >= 2:
                    e.wait_ge(s_up, 4 * (j - 1))
                for c in range(NKC):
                    e.dma_start(out=lnT[:, (j % 2) * 8192 + c * 512:(j % 2) * 8192 + (c + 1) * 512],
                                in_=ln0d.ap()[j * 512:(j + 1) * 512, c * 128:(c + 1) * 128],
                                transpose=True).then_inc(s_lnT[j % 2], 16)
                for sub in range(4):
                    s = j * 4 + sub
                    e.wait_ge(s_up, s + 1)
                    e.activation(vtan[:], pu[:], AF.Tanh).then_inc(s_ut, 1)
            for j in range(NB):
                e.wait_ge(s_gd[0], 16 * ((4 * (j + 1) + 1) // 2))
                e.wait_ge(s_gd[1], 16 * (4 * (j + 1) // 2))
                if j <= 1:
                    e.wait_ge(s_up, NT)
                if j >= 2:
                    e.wait_ge(s_wo, 4 * (j - 1))
                for c in range(NKC):
                    e.dma_start(out=lnT[:, (j % 2) * 8192 + c * 512:(j % 2) * 8192 + (c + 1) * 512],
                                in_=gatd.ap()[j * 512:(j + 1) * 512, c * 128:(c + 1) * 128],
                                transpose=True).then_inc(s_gT[j % 2], 16)

        @block.tensor
        def _(e):
            e.wait_ge(s_w, 16 * (NKC + 4))
            e.wait_ge(s_wh, 1)
            for s in range(NT):
                j = s // 4
                sub = s % 4
                e.wait_ge(s_lnT[j % 2], 256 * (j // 2 + 1))
                if s >= 1:
                    e.wait_ge(s_gat, s)
                base = (j % 2) * 8192
                for half in range(4):
                    p = pu[:, half * 512:(half + 1) * 512]
                    for c in range(NKC):
                        e.matmul(p, lnT[:, base + c * 512 + sub * 128:base + c * 512 + (sub + 1) * 128],
                                 wB[:, c * D + half * 512:c * D + (half + 1) * 512],
                                 start=(c == 0), stop=False)
                    mm = e.matmul(p, onesr[:], bu_s[:, half * 512:(half + 1) * 512],
                                  start=False, stop=True)
                    if half == 3:
                        mm.then_inc(s_up, 1)
            e.wait_ge(s_w2, 16 * NKC)
            for s in range(NT):
                j = s // 4
                sub = s % 4
                e.wait_ge(s_gT[j % 2], 256 * (j // 2 + 1))
                if s >= 1:
                    e.wait_ge(s_res, s)
                base = (j % 2) * 8192
                for half in range(4):
                    p = pw[:, half * 512:(half + 1) * 512]
                    for c in range(NKC):
                        e.matmul(p, lnT[:, base + c * 512 + sub * 128:base + c * 512 + (sub + 1) * 128],
                                 wB[:, c * D + half * 512:c * D + (half + 1) * 512],
                                 start=(c == 0), stop=False)
                    mm = e.matmul(p, onesr[:], bo_s[:, half * 512:(half + 1) * 512],
                                  start=False, stop=True)
                    if half == 3:
                        mm.then_inc(s_wo, 1)
    st.close()
    return nc




FFN_I = 7168
FFN_IB = 896
FFN_NIB = FFN_I // FFN_IB
FFN_NICB = FFN_IB // 128

def build_l0ffn(Tp=CTOK):
    nc = bass.Bass("TRN2", target_bir_lowering=False)
    x1o = nc.dram_tensor("x1o", [Tp, D], F32, kind="ExternalInput").ap()
    cWT = nc.dram_tensor("cWT", [D, 8], F32, kind="ExternalInput").ap()
    cbm_d = nc.dram_tensor("cb16", [1, 8], F32, kind="ExternalInput").ap()
    wmap_d = nc.dram_tensor("wmap", [128, 8], F32, kind="ExternalInput").ap()
    bg_d = nc.dram_tensor("bgate", [D, FFN_I], F32, kind="ExternalInput").ap()
    bu_d = nc.dram_tensor("bup", [D, FFN_I], F32, kind="ExternalInput").ap()
    bd_d = nc.dram_tensor("bdown", [FFN_I, D], F32, kind="ExternalInput").ap()
    gA_d = nc.dram_tensor("gA", [8, D, 64], F32, kind="ExternalInput").ap()
    gB_d = nc.dram_tensor("gB", [8, 64, FFN_I], F32, kind="ExternalInput").ap()
    uA_d = nc.dram_tensor("uA", [8, D, 64], F32, kind="ExternalInput").ap()
    uB_d = nc.dram_tensor("uB", [8, 64, FFN_I], F32, kind="ExternalInput").ap()
    dA_d = nc.dram_tensor("dA", [8, FFN_I, 64], F32, kind="ExternalInput").ap()
    dB_d = nc.dram_tensor("dB", [8, 64, D], F32, kind="ExternalInput").ap()
    sel_d = nc.dram_tensor("selmat", [8, 8 * 128], F32, kind="ExternalInput").ap()
    id_d = nc.dram_tensor("ident", [128, 128], F32, kind="ExternalInput").ap()
    x2 = nc.dram_tensor("x2", [Tp, D], F32, kind="ExternalOutput").ap()
    ln1d = nc.dram_tensor("ln1d", [Tp, D], BF16)
    bgd = nc.dram_tensor("bgd", [Tp, FFN_I], BF16)
    bud = nc.dram_tensor("bud", [Tp, FFN_I], BF16)
    hsd = nc.dram_tensor("hsd", [FFN_I, Tp], BF16)

    NT = Tp // 128
    NTB = Tp // 512
    NKC = D // 128
    NI5 = FFN_I // 512
    NIC = FFN_I // 128
    NIB, NICB, IB = FFN_NIB, FFN_NICB, FFN_IB
    NTS_G = min(8, NT)
    NTG = NT // NTS_G
    NGI = NIB * 8 * NICB * NTB

    from contextlib import ExitStack
    st = ExitStack()
    def sb(*a, **k):
        return st.enter_context(nc.sbuf_tensor(*a, **k))
    def ps(*a, **k):
        return st.enter_context(nc.psum_tensor(*a, **k))
    def sem(n, k=1):
        if k == 1:
            return st.enter_context(nc.semaphore(n))
        return [st.enter_context(nc.semaphore(f"{n}{i}")) for i in range(k)]

    xt = sb("xt", [128, 2 * D], F32)
    bst = sb("bst", [128, 24], F32)
    mv = sb("mv", [128, 2], F32)
    nmu = sb("nmu", [128, 1], F32)
    veps = sb("veps", [128, 1], F32)
    rstd = sb("rstd", [128, 1], F32)
    tmp1 = sb("tmp1", [128, 1], F32)
    ln1t = sb("ln1t", [128, 2 * D], BF16)
    ebuf = sb("ebuf", [128, 4 * 512], BF16)
    cwts = sb("cwts", [128, NKC * 8], BF16)
    cb16 = sb("cb16s", [1, 8], BF16)
    wmp = sb("wmp", [128, 8], F32)
    selm = sb("selm", [8, 8 * 128], BF16)
    i128 = sb("i128", [128, 128], BF16)
    wes = sb("wes", [128, NT * 8], F32)
    we16 = sb("we16", [128, 8], BF16)
    weT = sb("weT", [8, Tp], BF16)
    rz = sb("rz", [128, 8], F32)
    rtan = sb("rtan", [128, 8], BF16)
    rb = sb("rb", [128, 8], F32)
    rm1 = sb("rm1", [128, 1], F32)
    rm2 = sb("rm2", [128, 1], F32)
    ris1 = sb("ris1", [128, 8], F32)
    ris2 = sb("ris2", [128, 8], F32)
    rw1 = sb("rw1", [128, 1], F32)
    rw1t = sb("rw1t", [128, 1], BF16)
    rmsk = sb("rmsk", [128, 8], F32)
    aab = sb("aab", [128, 2 * NKC * 64], BF16)
    xAT = sb("xAT", [128, 8 * Tp], BF16)
    dzacc = sb("dzacc", [128, 4 * Tp], BF16)
    wB2 = sb("wB2", [128, 2 * 2 * IB], BF16)
    dAs = sb("dAs", [128, 2 * NICB * 64], BF16)
    wem = sb("wem", [128, Tp], BF16)
    sg = sb("sg", [128, 512], BF16)
    hbuf = sb("hbuf", [128, 2 * 512], BF16)
    onesr = sb("onesr", [1, 128], BF16)
    AW_LNT = NKC * Tp // 2
    AW_A = AW_LNT + 16 * 512
    AW_B = NICB * Tp + NICB * Tp // 2
    AW_C = NTS_G * D + 512 + 1024 + 2 * D + 2 * D
    ar_words = max(AW_A, AW_B, AW_C)
    arena = sb("arena", [128, ar_words], F32, is_transpose_dest=True)
    lnT = arena[:, 0:AW_LNT].bitcast(BF16)
    wcols = arena[:, AW_LNT:AW_A].bitcast(BF16)
    bgT = arena[:, 0:NICB * Tp // 2].bitcast(BF16)
    buT = arena[:, NICB * Tp // 2:NICB * Tp].bitcast(BF16)
    hsum = arena[:, NICB * Tp:AW_B].bitcast(BF16)
    otile = arena[:, 0:NTS_G * D]
    bdt = arena[:, NTS_G * D:NTS_G * D + 512].bitcast(BF16)
    hsTt = arena[:, NTS_G * D + 512:NTS_G * D + 512 + 1024].bitcast(BF16)
    res = arena[:, NTS_G * D + 512 + 1024:NTS_G * D + 512 + 1024 + 2 * D]
    dBp = arena[:, NTS_G * D + 512 + 1024 + 2 * D:AW_C].bitcast(BF16)

    pALL = ps("pALL", [128, 8 * 512], F32)
    def bank(k):
        return pALL[:, k * 512:(k + 1) * 512]

    s_w = sem("s_w")
    s_x = sem("s_x", 2)
    s_ln = sem("s_ln")
    s_lnd = sem("s_lnd", 2)
    s_lnT = sem("s_lnT", 8)
    s_aab = sem("s_aab", 2)
    s_rt = sem("s_rt")
    s_rtn = sem("s_rtn")
    s_rwe = sem("s_rwe")
    s_wet = sem("s_wet")
    s_wetv = sem("s_wetv")
    s_xA = sem("s_xA")
    s_xAv = sem("s_xAv")
    s_wc = sem("s_wc", 2)
    s_base = sem("s_base")
    s_bev = sem("s_bev")
    s_bsd = sem("s_bsd", 4)
    s_phA = sem("s_phA")
    s_bgT = sem("s_bgT")
    s_wl = sem("s_wl", 2)
    s_wm = sem("s_wm")
    s_wmv = sem("s_wmv")
    s_g = sem("s_g")
    s_u = sem("s_u")
    s_sg = sem("s_sg")
    s_h = sem("s_h")
    s_dz = sem("s_dz")
    s_dzv = sem("s_dzv")
    s_hs = sem("s_hs")
    s_dz16 = sem("s_dz16")
    s_phB = sem("s_phB")
    s_bd = sem("s_bd", 2)
    s_dB = sem("s_dB", 2)
    s_hT = sem("s_hT", 2)
    s_ct = sem("s_ct")
    s_mm = sem("s_mm")
    s_ev = sem("s_ev")
    s_x1r = sem("s_x1r", 2)
    s_res = sem("s_res")
    s_st = sem("s_st", 2)
    block = st.enter_context(nc.Block())
    if True:
        @block.sync
        def _(e):
            for i in range(NT):
                if i >= 2:
                    e.wait_ge(s_ln, i - 1)
                e.dma_start(out=xt[:, (i % 2) * D:(i % 2 + 1) * D],
                            in_=x1o[i * 128:(i + 1) * 128, :]).then_inc(s_x[i % 2], 16)
            for tg in range(NTG):
                for d4 in range(4):
                    g = tg * 4 + d4
                    for ic in range(NIC):
                        idx = g * NIC + ic
                        if idx == 0:
                            e.wait_ge(s_hs, 16 * NICB * NIB)
                        if idx >= 2:
                            e.wait_ge(s_ct, idx - 1)
                        e.dma_start(out=hsTt[:, (idx % 2) * 1024:(idx % 2) * 1024 + NTS_G * 128],
                                    in_=hsd.ap()[ic * 128:(ic + 1) * 128,
                                                 tg * NTS_G * 128:(tg + 1) * NTS_G * 128]
                                    ).then_inc(s_hT[idx % 2], 16)
                for ts in range(NTS_G):
                    k = tg * NTS_G + ts
                    if k == 0:
                        e.wait_ge(s_ln, NT)
                    if k >= 2:
                        e.wait_ge(s_res, k - 1)
                    e.dma_start(out=xt[:, (k % 2) * D:(k % 2 + 1) * D],
                                in_=x1o[k * 128:(k + 1) * 128, :]).then_inc(s_x1r[k % 2], 16)
                    if k >= 2:
                        e.dma_start(out=x2[(k - 2) * 128:(k - 1) * 128, :],
                                    in_=res[:, ((k - 2) % 2) * D:((k - 2) % 2 + 1) * D]
                                    ).then_inc(s_st[(k - 2) % 2], 16)
            for k in range(NT, NT + 2):
                e.wait_ge(s_res, k - 1)
                e.dma_start(out=x2[(k - 2) * 128:(k - 1) * 128, :],
                            in_=res[:, ((k - 2) % 2) * D:((k - 2) % 2 + 1) * D]
                            ).then_inc(s_st[(k - 2) % 2], 16)
            e.wait_ge(s_st[0], 16 * ((NT + 1) // 2))
            e.wait_ge(s_st[1], 16 * (NT // 2))

        @block.gpsimd
        def _(e):
            nw = 0
            for c in range(NKC):
                e.dma_start(out=cwts[:, c * 8:(c + 1) * 8],
                            in_=cWT[c * 128:(c + 1) * 128, :]).then_inc(s_w, 16); nw += 1
            e.dma_start(out=cb16[:], in_=cbm_d[:, :]).then_inc(s_w, 16); nw += 1
            e.dma_start(out=wmp[:], in_=wmap_d[:, :]).then_inc(s_w, 16); nw += 1
            e.dma_start(out=selm[:], in_=sel_d[:, :]).then_inc(s_w, 16); nw += 1
            e.dma_start(out=i128[:], in_=id_d[:, :]).then_inc(s_w, 16); nw += 1
            for i in range(NT):
                e.wait_ge(s_ln, i + 1)
                e.dma_start(out=ln1d.ap()[i * 128:(i + 1) * 128, :],
                            in_=ln1t[:, (i % 2) * D:(i % 2 + 1) * D]).then_inc(s_lnd[i % 2], 16)
            for k in range(16):
                eidx, gu = k // 2, k % 2
                if k >= 2:
                    e.wait_ge(s_xA, (k - 1) * NTB)
                src = (gA_d if gu == 0 else uA_d)[eidx].rearrange("(c p) r -> p c r", p=128)
                dst = aab[:, (k % 2) * NKC * 64:(k % 2 + 1) * NKC * 64].rearrange(
                    "p (c r) -> p c r", c=NKC)
                e.dma_start(out=dst, in_=src).then_inc(s_aab[k % 2], 16)
            for idx2 in range(NI5 * 2 + 1):
                if idx2 <= NI5 * 2 - 1:
                    i5, gu = idx2 // 2, idx2 % 2
                    if idx2 >= 2:
                        e.wait_ge(s_base, (idx2 - 1) * NT)
                    src = (bg_d if gu == 0 else bu_d)
                    for c in range(NKC):
                        e.dma_start(out=wcols[:, (idx2 % 2) * 16 * 512 + c * 512:
                                              (idx2 % 2) * 16 * 512 + (c + 1) * 512],
                                    in_=src[c * 128:(c + 1) * 128, i5 * 512:(i5 + 1) * 512]
                                    ).then_inc(s_wc[idx2 % 2], 16)
                if idx2 >= 1:
                    i5p, gup = (idx2 - 1) // 2, (idx2 - 1) % 2
                    tgt = (bgd if gup == 0 else bud).ap()
                    for s in range(NT):
                        idx = (idx2 - 1) * NT + s
                        e.wait_ge(s_bev, idx + 1)
                        e.dma_start(out=tgt[s * 128:(s + 1) * 128, i5p * 512:(i5p + 1) * 512],
                                    in_=ebuf[:, (idx % 4) * 512:(idx % 4 + 1) * 512]
                                    ).then_inc(s_bsd[idx % 4], 16)
            for ib in range(NIB):
                for eidx in range(8):
                    k = ib * 8 + eidx
                    if k >= 2:
                        e.wait_ge(s_dz, (k - 1) * NICB * NTB)
                    pb = (eidx % 2) * 64
                    e.dma_start(out=wB2[pb:pb + 64, (k % 2) * 2 * IB:(k % 2) * 2 * IB + IB],
                                in_=gB_d[eidx, :, ib * IB:(ib + 1) * IB]).then_inc(s_wl[k % 2], 16)
                    e.dma_start(out=wB2[pb:pb + 64, (k % 2) * 2 * IB + IB:(k % 2 + 1) * 2 * IB],
                                in_=uB_d[eidx, :, ib * IB:(ib + 1) * IB]).then_inc(s_wl[k % 2], 16)
                    e.dma_start(out=dAs[:, (k % 2) * NICB * 64:(k % 2 + 1) * NICB * 64].rearrange(
                                    "p (c r) -> p c r", c=NICB),
                                in_=dA_d[eidx, ib * IB:(ib + 1) * IB, :].rearrange("(c p) r -> p c r", p=128)
                                ).then_inc(s_wl[k % 2], 16)
                e.wait_ge(s_h, (ib + 1) * 8 * NICB * NTB)
                for c in range(NICB):
                    e.dma_start(out=hsd.ap()[ib * IB + c * 128:ib * IB + (c + 1) * 128, :],
                                in_=hsum[:, c * Tp:(c + 1) * Tp]).then_inc(s_hs, 16)
            for eidx in range(8):
                e.dma_start(out=dBp[(eidx % 2) * 64:(eidx % 2) * 64 + 64,
                                    (eidx // 2) * D:(eidx // 2 + 1) * D],
                            in_=dB_d[eidx]).then_inc(s_dB[eidx % 2], 16)
            for tg in range(NTG):
                for d4 in range(4):
                    g2 = tg * 4 + d4
                    for ic in range(NIC):
                        idx = g2 * NIC + ic
                        if idx == 0:
                            e.wait_ge(s_hs, 16 * NICB * NIB)
                        if idx >= 2:
                            e.wait_ge(s_ct, idx - 1)
                        e.dma_start(out=bdt[:, (idx % 2) * 512:(idx % 2 + 1) * 512],
                                    in_=bd_d[ic * 128:(ic + 1) * 128, d4 * 512:(d4 + 1) * 512]
                                    ).then_inc(s_bd[idx % 2], 16)

        @block.vector
        def _(e):
            e = AutoDrain(e)
            e.memset(onesr[:], 1.0)
            for i in range(NT):
                e.wait_ge(s_x[i % 2], 16 * (i // 2 + 1))
                if i >= 2:
                    e.wait_ge(s_lnd[i % 2], 16 * (i // 2))
                xs = xt[:, (i % 2) * D:(i % 2 + 1) * D]
                emit_ln_stats(e, bst, mv, nmu, veps, rstd, tmp1, xs)
                e.tensor_scalar(ln1t[:, (i % 2) * D:(i % 2 + 1) * D], xs,
                                nmu[:], rstd[:], OP.add, OP.mult).then_inc(s_ln, 1)
            for i in range(NT):
                e.wait_ge(s_rt, i + 1)
                e.tensor_copy(rz[:], bank(4)[:, 0:8]).then_inc(s_rwe, 1)
                e.wait_ge(s_rtn, 2 * i + 1)
                e.scalar_tensor_tensor(rb[:], rtan[:], 0.5, wmp[:], OP.mult, OP.mult)
                e.scalar_tensor_tensor(rb[:], wmp[:], 0.5, rb[:], OP.mult, OP.add)
                e.tensor_reduce(rm1[:], rb[:], mybir.AxisListType.X, OP.max)
                e.tensor_scalar(ris1[:], rb[:], rm1[:], None, OP.is_ge)
                e.scalar_tensor_tensor(rmsk[:], ris1[:], -1e30, rb[:], OP.mult, OP.add)
                e.tensor_reduce(rm2[:], rmsk[:], mybir.AxisListType.X, OP.max)
                e.tensor_scalar(ris2[:], rmsk[:], rm2[:], None, OP.is_ge)
                e.tensor_tensor(tmp1[:], rm1[:], rm2[:], OP.subtract).then_inc(s_rwe, 1)
                e.wait_ge(s_rtn, 2 * i + 2)
                e.tensor_scalar(rw1[:], rw1t[:], 0.5, 0.5, OP.mult, OP.add)
                wsl = wes[:, i * 8:(i + 1) * 8]
                e.tensor_scalar(wsl, ris1[:], rw1[:], None, OP.mult)
                e.tensor_scalar(tmp1[:], rw1[:], -1.0, 1.0, OP.mult, OP.add)
                e.scalar_tensor_tensor(wsl, ris2[:], tmp1[:], wsl, OP.mult, OP.add)
                e.tensor_copy(we16[:], wsl).then_inc(s_rwe, 1)
                e.wait_ge(s_wet, i + 1)
                e.tensor_copy(weT[:, i * 128:(i + 1) * 128],
                              pALL[0:8, 5 * 512:5 * 512 + 64].bitcast(BF16)).then_inc(s_wetv, 1)
            for k in range(16):
                eidx, gu = k // 2, k % 2
                pb = (eidx % 2) * 64
                slot = (eidx // 2) * 2 + gu
                for j in range(NTB):
                    e.wait_ge(s_xA, k * NTB + j + 1)
                    e.tensor_scalar(xAT[pb:pb + 64, slot * Tp + j * 512:slot * Tp + (j + 1) * 512],
                                    pALL[0:64, 6 * 512:7 * 512],
                                    0.25, None, OP.mult).then_inc(s_xAv, 1)
            for ib in range(NIB):
                for eidx in range(8):
                    k = ib * 8 + eidx
                    pb = (eidx % 2) * 64
                    for t4 in range(NTB):
                        e.wait_ge(s_wm, k * NTB + t4 + 1)
                        e.tensor_copy(wem[:, t4 * 512:(t4 + 1) * 512],
                                      bank(0)).then_inc(s_wmv, 1)
                    for ic in range(NICB):
                        for t4 in range(NTB):
                            gi = (k * NICB + ic) * NTB + t4
                            e.wait_ge(s_sg, gi + 1)
                            e.wait_ge(s_u, gi + 1)
                            if gi >= 2:
                                e.wait_ge(s_dz, gi - 1)
                            if eidx == 0 and ic == 0 and t4 == 0 and ib >= 1:
                                e.wait_ge(s_hs, 16 * NICB * ib)
                            hb = hbuf[:, (gi % 2) * 512:(gi % 2 + 1) * 512]
                            e.tensor_tensor(hb, sg[:], bank(2 + (gi % 2)), OP.mult)
                            e.tensor_tensor(hb, hb, wem[:, t4 * 512:(t4 + 1) * 512], OP.mult)
                            hss = hsum[:, ic * Tp + t4 * 512:ic * Tp + (t4 + 1) * 512]
                            if eidx == 0:
                                e.tensor_copy(hss, hb).then_inc(s_h, 1)
                            else:
                                e.tensor_tensor(hss, hss, hb, OP.add).then_inc(s_h, 1)
                    for t4 in range(NTB):
                        e.wait_ge(s_dz, (k * NICB + NICB - 1) * NTB + t4 + 1)
                        dsl = dzacc[pb:pb + 64, (eidx // 2) * Tp + t4 * 512:(eidx // 2) * Tp + (t4 + 1) * 512]
                        if ib == 0:
                            e.tensor_scalar(dsl, pALL[pb:pb + 64, (4 + t4) * 512:(5 + t4) * 512],
                                            0.25, None, OP.mult).then_inc(s_dzv, 1)
                        else:
                            e.scalar_tensor_tensor(dsl, pALL[pb:pb + 64, (4 + t4) * 512:(5 + t4) * 512],
                                                   0.25, dsl, OP.mult, OP.add).then_inc(s_dzv, 1)
            for k in range(NT):
                tg, ts = k // NTS_G, k % NTS_G
                e.wait_ge(s_ev, tg * 4 + 4)
                e.wait_ge(s_x1r[k % 2], 16 * (k // 2 + 1))
                if k >= 2:
                    e.wait_ge(s_st[k % 2], 16 * (k // 2))
                osl = otile[:, ts * D:(ts + 1) * D]
                emit_ln_stats(e, bst, mv, nmu, veps, rstd, tmp1, osl)
                e.tensor_scalar(res[:, (k % 2) * D:(k % 2 + 1) * D], osl,
                                nmu[:], rstd[:], OP.add, OP.mult)
                e.tensor_tensor(res[:, (k % 2) * D:(k % 2 + 1) * D],
                                res[:, (k % 2) * D:(k % 2 + 1) * D],
                                xt[:, (k % 2) * D:(k % 2 + 1) * D],
                                OP.add).then_inc(s_res, 1)

        @block.scalar
        def _(e):
            e = AutoDrain(e)
            for j in range(NTB):
                e.wait_ge(s_lnd[0], 16 * ((4 * (j + 1) + 1) // 2))
                e.wait_ge(s_lnd[1], 16 * (4 * (j + 1) // 2))
                for c in range(NKC):
                    e.dma_start(out=lnT[:, c * Tp + j * 512:c * Tp + (j + 1) * 512],
                                in_=ln1d.ap()[j * 512:(j + 1) * 512, c * 128:(c + 1) * 128],
                                transpose=True).then_inc(s_lnT[j], 16)
            for i in range(NT):
                e.wait_ge(s_rwe, 3 * i + 1)
                e.activation(rtan[:], rz[:], AF.Tanh, scale=0.5).then_inc(s_rtn, 1)
                e.wait_ge(s_rwe, 3 * i + 2)
                e.activation(rw1t[:], tmp1[:], AF.Tanh, scale=0.5).then_inc(s_rtn, 1)
            for idx in range(NI5 * 2 * NT):
                e.wait_ge(s_base, idx + 1)
                if idx >= 4:
                    e.wait_ge(s_bsd[idx % 4], 16 * (idx // 4))
                e.activation(ebuf[:, (idx % 4) * 512:(idx % 4 + 1) * 512],
                             bank(idx % 4), AF.Copy).then_inc(s_bev, 1)
            for ib in range(NIB):
                if ib == 0:
                    e.wait_ge(s_phA, 1)
                    for q in range(4):
                        e.wait_ge(s_bsd[q], 16 * (NI5 * 2 * NT // 4))
                else:
                    e.wait_ge(s_u, ib * 8 * NICB * NTB)
                for c in range(NICB):
                    for t4 in range(NTB):
                        e.dma_start(out=bgT[:, c * Tp + t4 * 512:c * Tp + (t4 + 1) * 512],
                                    in_=bgd.ap()[t4 * 512:(t4 + 1) * 512,
                                                 ib * IB + c * 128:ib * IB + (c + 1) * 128],
                                    transpose=True).then_inc(s_bgT, 16)
                        e.dma_start(out=buT[:, c * Tp + t4 * 512:c * Tp + (t4 + 1) * 512],
                                    in_=bud.ap()[t4 * 512:(t4 + 1) * 512,
                                                 ib * IB + c * 128:ib * IB + (c + 1) * 128],
                                    transpose=True).then_inc(s_bgT, 16)
                for eidx in range(8):
                    k = ib * 8 + eidx
                    for ic in range(NICB):
                        for t4 in range(NTB):
                            gi = (k * NICB + ic) * NTB + t4
                            e.wait_ge(s_g, gi + 1)
                            if gi >= 1:
                                e.wait_ge(s_h, gi)
                            e.activation(sg[:], bank(gi % 2), AF.Silu).then_inc(s_sg, 1)
            for tg in range(NTG):
                for d4 in range(4):
                    g = tg * 4 + d4
                    e.wait_ge(s_mm, g + 1)
                    if d4 == 0 and tg >= 1:
                        e.wait_ge(s_res, tg * NTS_G)
                    for ts in range(NTS_G):
                        e.activation(otile[:, ts * D + d4 * 512:ts * D + (d4 + 1) * 512],
                                     bank(ts), AF.Copy)
                    e.nop().then_inc(s_ev, 1)

        @block.tensor
        def _(e):
            e.wait_ge(s_w, 16 * (NKC + 4))
            for j in range(NTB):
                e.wait_ge(s_lnT[j], 256)
                for sub in range(4):
                    i = j * 4 + sub
                    if i >= 1:
                        e.wait_ge(s_rwe, 3 * (i - 1) + 1)
                    for c in range(NKC):
                        e.matmul(bank(4)[:, 0:8], lnT[:, c * Tp + i * 128:c * Tp + i * 128 + 128],
                                 cwts[:, c * 8:(c + 1) * 8], start=(c == 0), stop=False)
                    e.matmul(bank(4)[:, 0:8], onesr[:], cb16[:],
                             start=False, stop=True).then_inc(s_rt, 1)
                    e.wait_ge(s_rwe, 3 * i + 3)
                    if i >= 1:
                        e.wait_ge(s_wetv, i)
                    e.transpose(pALL[0:8, 5 * 512:5 * 512 + 64].bitcast(BF16),
                                we16[:], i128[:]).then_inc(s_wet, 1)
            for k in range(16):
                e.wait_ge(s_aab[k % 2], 16 * (k // 2 + 1))
                for j in range(NTB):
                    if k * NTB + j >= 1:
                        e.wait_ge(s_xAv, k * NTB + j)
                    for c in range(NKC):
                        mm = e.matmul(pALL[0:64, 6 * 512:7 * 512],
                                      aab[:, (k % 2) * NKC * 64 + c * 64:(k % 2) * NKC * 64 + (c + 1) * 64],
                                      lnT[:, c * Tp + j * 512:c * Tp + (j + 1) * 512],
                                      start=(c == 0), stop=(c == NKC - 1))
                    mm.then_inc(s_xA, 1)
            for i5 in range(NI5):
                for gu in range(2):
                    idx2 = i5 * 2 + gu
                    e.wait_ge(s_wc[idx2 % 2], 16 * 16 * (idx2 // 2 + 1))
                    for s in range(NT):
                        idx = idx2 * NT + s
                        if idx >= 4:
                            e.wait_ge(s_bev, idx - 3)
                        for c in range(NKC):
                            mm = e.matmul(bank(idx % 4),
                                          lnT[:, c * Tp + s * 128:c * Tp + (s + 1) * 128],
                                          wcols[:, (idx2 % 2) * 16 * 512 + c * 512:
                                                (idx2 % 2) * 16 * 512 + (c + 1) * 512],
                                          start=(c == 0), stop=(c == NKC - 1))
                        mm.then_inc(s_base, 1)
            e.nop().then_inc(s_phA, 1)
            for ib in range(NIB):
                for eidx in range(8):
                    k = ib * 8 + eidx
                    pb = (eidx % 2) * 64
                    e.wait_ge(s_bgT, 16 * (ib + 1) * NICB * NTB * 2)
                    e.wait_ge(s_wl[k % 2], 16 * 3 * (k // 2 + 1))
                    for t4 in range(NTB):
                        if k * NTB + t4 >= 1:
                            e.wait_ge(s_wmv, k * NTB + t4)
                        if t4 == 0 and k >= 1:
                            e.wait_ge(s_sg, k * NICB * NTB)
                            e.wait_ge(s_h, k * NICB * NTB)
                        e.matmul(bank(0), selm[0:8, eidx * 128:(eidx + 1) * 128],
                                 weT[:, t4 * 512:(t4 + 1) * 512],
                                 start=True, stop=True).then_inc(s_wm, 1)
                    e.wait_ge(s_wmv, (k + 1) * NTB)
                    for ic in range(NICB):
                        for t4 in range(NTB):
                            gi = (k * NICB + ic) * NTB + t4
                            if gi >= 2:
                                e.wait_ge(s_sg, gi - 1)
                                e.wait_ge(s_h, gi - 1)
                            gb = bank(gi % 2)
                            ub = bank(2 + (gi % 2))
                            gslot = (eidx // 2) * 2
                            e.matmul(gb, wB2[pb:pb + 64, (k % 2) * 2 * IB + ic * 128:(k % 2) * 2 * IB + (ic + 1) * 128],
                                     xAT[pb:pb + 64, gslot * Tp + t4 * 512:gslot * Tp + (t4 + 1) * 512],
                                     start=True, stop=False)
                            e.matmul(gb, i128[:], bgT[:, ic * Tp + t4 * 512:ic * Tp + (t4 + 1) * 512],
                                     start=False, stop=True).then_inc(s_g, 1)
                            e.matmul(ub, wB2[pb:pb + 64, (k % 2) * 2 * IB + IB + ic * 128:(k % 2) * 2 * IB + IB + (ic + 1) * 128],
                                     xAT[pb:pb + 64, (gslot + 1) * Tp + t4 * 512:(gslot + 1) * Tp + (t4 + 1) * 512],
                                     start=True, stop=False)
                            e.matmul(ub, i128[:], buT[:, ic * Tp + t4 * 512:ic * Tp + (t4 + 1) * 512],
                                     start=False, stop=True).then_inc(s_u, 1)
                            e.wait_ge(s_h, gi + 1)
                            if ic == 0 and k >= 1:
                                e.wait_ge(s_dzv, k * NTB)
                            e.matmul(pALL[pb:pb + 64, (4 + t4) * 512:(5 + t4) * 512],
                                     dAs[:, (k % 2) * NICB * 64 + ic * 64:(k % 2) * NICB * 64 + (ic + 1) * 64],
                                     hbuf[:, (gi % 2) * 512:(gi % 2 + 1) * 512],
                                     start=(ic == 0), stop=(ic == NICB - 1)).then_inc(s_dz, 1)
            e.wait_ge(s_dzv, NIB * 8 * NTB)
            dz16v = dzacc
            for tg in range(NTG):
                for d4 in range(4):
                    g = tg * 4 + d4
                    if g >= 1:
                        e.wait_ge(s_ev, g)
                    for ic in range(NIC):
                        idx = g * NIC + ic
                        e.wait_ge(s_bd[idx % 2], 16 * (idx // 2 + 1))
                        e.wait_ge(s_hT[idx % 2], 16 * (idx // 2 + 1))
                        for ts in range(NTS_G):
                            mm = e.matmul(bank(ts),
                                          hsTt[:, (idx % 2) * 1024 + ts * 128:(idx % 2) * 1024 + (ts + 1) * 128],
                                          bdt[:, (idx % 2) * 512:(idx % 2 + 1) * 512],
                                          start=(ic == 0), stop=False)
                            if ts == NTS_G - 1:
                                mm.then_inc(s_ct, 1)
                    if g == 0:
                        e.wait_ge(s_dB[0], 64)
                        e.wait_ge(s_dB[1], 64)
                    for eidx in range(8):
                        pb = (eidx % 2) * 64
                        for ts in range(NTS_G):
                            tok = tg * NTS_G * 128 + ts * 128
                            mm = e.matmul(bank(ts),
                                          dz16v[pb:pb + 64, (eidx // 2) * Tp + tok:(eidx // 2) * Tp + tok + 128],
                                          dBp[pb:pb + 64, (eidx // 2) * D + d4 * 512:(eidx // 2) * D + (d4 + 1) * 512],
                                          start=False, stop=(eidx == 7))
                            if eidx == 7 and ts == NTS_G - 1:
                                mm.then_inc(s_mm, 1)
    st.close()
    return nc



L1_NPC = 3584
L1_CV = 2560

def build_l1attn(Lq=L, dbg=False):
    nc = bass.Bass("TRN2", target_bir_lowering=False)
    x2b = nc.dram_tensor("x2b", [Lq, D], F32, kind="ExternalInput").ap()
    Wall_d = nc.dram_tensor("Wall", [D, L1_NPC], F32, kind="ExternalInput").ap()
    cw_d = nc.dram_tensor("cw", [L1_CV, 4], F32, kind="ExternalInput").ap()
    alog_d = nc.dram_tensor("alog", [128, 4], F32, kind="ExternalInput").ap()
    dtb_d = nc.dram_tensor("dtb", [128, 4], F32, kind="ExternalInput").ap()
    nwm_d = nc.dram_tensor("nwm", [128, 128], F32, kind="ExternalInput").ap()
    cel_d = nc.dram_tensor("cel", [128, 128], F32, kind="ExternalInput").ap()
    ceh_d = nc.dram_tensor("ceh", [128, 128], F32, kind="ExternalInput").ap()
    bel_d = nc.dram_tensor("bel", [128, 2 * 128], F32, kind="ExternalInput").ap()
    beh_d = nc.dram_tensor("beh", [128, 2 * 128], F32, kind="ExternalInput").ap()
    odd_d = nc.dram_tensor("oddsel", [128, 64], F32, kind="ExternalInput").ap()
    m2t_d = nc.dram_tensor("m2t", [128, 64], F32, kind="ExternalInput").ap()
    mstr_d = nc.dram_tensor("mstr", [128, 128], F32, kind="ExternalInput").ap()
    id_d = nc.dram_tensor("identt", [128, 128], F32, kind="ExternalInput").ap()
    og_d = nc.dram_tensor("og", [Lq, 512], BF16, kind="ExternalOutput").ap()
    ln2d = nc.dram_tensor("ln2d", [Lq, D], BF16)
    projd = nc.dram_tensor("projd", [Lq, L1_NPC], BF16)
    convd = nc.dram_tensor("convd", [L1_CV, Lq], BF16)
    dbgd = nc.dram_tensor("dbgd", [Lq // 64, 6 * 128, 128], F32) if dbg else None

    NT = Lq // 128
    NTB = Lq // 512
    NKC = D // 128
    NP5 = L1_NPC // 512
    NCC = L1_CV // 128
    NCH = Lq // 64
    SQD = 0.08838834764831845

    from contextlib import ExitStack
    st = ExitStack()
    def sb(*a, **k):
        return st.enter_context(nc.sbuf_tensor(*a, **k))
    def ps(*a, **k):
        return st.enter_context(nc.psum_tensor(*a, **k))
    def sem(n, k=1):
        if k == 1:
            return st.enter_context(nc.semaphore(n))
        return [st.enter_context(nc.semaphore(f"{n}{i}")) for i in range(k)]

    xt = sb("xt", [128, 2 * D], F32)
    bst = sb("bst", [128, 24], F32)
    mv = sb("mv", [128, 2], F32)
    nmu = sb("nmu", [128, 1], F32)
    veps = sb("veps", [128, 1], F32)
    rstd = sb("rstd", [128, 1], F32)
    tmp1 = sb("tmp1", [128, 1], F32)
    ln2t = sb("ln2t", [128, 2 * D], BF16)
    ebuf = sb("ebuf", [128, 4 * 512], BF16)
    onesr = sb("onesr", [1, 128], BF16)
    cw = sb("cw_s", [128, NCC * 4], F32)
    alog = sb("alog_s", [128, 4], F32)
    negea = sb("negea", [128, 4], F32)
    dtb = sb("dtb_s", [128, 4], F32)
    nwm = sb("nwm_s", [128, 128], BF16)
    cel = sb("cel_s", [128, 128], BF16)
    ceh = sb("ceh_s", [128, 128], BF16)
    bel = sb("bel_s", [128, 2 * 128], BF16)
    beh = sb("beh_s", [128, 2 * 128], BF16)
    oddsel = sb("oddsel_s", [128, 64], BF16)
    m2t = sb("m2t_s", [128, 64], F32)
    mstr = sb("mstr_s", [128, 128], F32)
    i128 = sb("i128", [128, 128], BF16)
    bat = sb("bat", [128, NT * 12], BF16)
    zt3 = sb("zt3", [128, NT * 4], F32)
    zt4 = sb("zt4", [128, NT * 4], F32)
    barr = sb("barr", [128, NT * 8], BF16)
    garr = sb("garr", [128, NT * 4], BF16)
    S0 = sb("S0", [128, 128], BF16)
    S0f = sb("S0f", [128, 128], F32)
    kTr = sb("kTr", [128, 128], BF16)
    vTr = sb("vTr", [128, 128], BF16)
    kT = sb("kT", [128, 128], BF16)
    qT = sb("qT", [128, 64], BF16)
    Ktm = sb("Ktm", [128, 128], BF16)
    Vtm = sb("Vtm", [128, 128], BF16)
    Qtm = sb("Qtm", [64, 128], F32)
    Qtm16 = sb("Qtm16", [64, 128], BF16)
    sqf = sb("sqf", [128, 128], F32)
    rsc = sb("rsc", [128, 1], F32)
    rst2 = sb("rst2", [128, 1], F32)
    lcol = sb("lcol", [128, 1], F32)
    nlcol = sb("nlcol", [128, 1], F32)
    lc16 = sb("lc16", [128, 1], BF16)
    lrow = sb("lrow", [1, 128], BF16)
    bcolf = sb("bcolf", [128, 1], F32)
    nb16 = sb("nb16", [128, 1], BF16)
    brr = sb("brr", [1, 128], BF16)
    ldmm = sb("ldmm", [128, 128], F32)
    dtm = sb("dtm", [128, 128], F32)
    dtq = sb("dtq", [128, 64], F32)
    At = sb("At", [128, 128], F32)
    Pt = sb("Pt", [128, 128], F32)
    Pm = sb("Pm", [128, 128], F32)
    if32 = sb("if32", [128, 128], F32)
    attn = sb("attn", [128, 64], BF16)
    uf = sb("uf", [128, 128], F32)
    u16 = sb("u16", [128, 128], BF16)
    acol = sb("acol", [128, 1], F32)
    kpe = sb("kpe", [128, 1], F32)
    Kp16 = sb("Kp16", [128, 128], BF16)
    aCc = sb("aCc", [128, 1], F32)
    aoc = sb("aoc", [64, 1], F32)
    otile = sb("otile", [64, 128], F32)
    ot2 = sb("ot2", [64, 128], F32)
    osc = sb("osc", [64, 1], F32)

    AW_LNT = NKC * (Lq // 2) // 2
    AW_A = AW_LNT + 2 * 16 * 512 // 2
    PADW = 64
    AW_P1 = 2 * (PADW + Lq // 2)
    AW_P4 = 5 * Lq // 2 + 2 * (NCH * 128 // 2)
    ar_words = max(AW_A, AW_P1, AW_P4)
    arena = sb("arena", [128, ar_words], F32, is_transpose_dest=True)
    lnT = arena[:, 0:AW_LNT].bitcast(BF16)
    wcols = arena[:, AW_LNT:AW_A].bitcast(BF16)
    cvin = arena[:, 0:PADW + Lq // 2].bitcast(BF16)
    cvout = arena[:, PADW + Lq // 2:AW_P1].bitcast(BF16)
    qrow = arena[:, 0:Lq // 2].bitcast(BF16)
    k0row = arena[:, Lq // 2:Lq].bitcast(BF16)
    k1row = arena[:, Lq:3 * Lq // 2].bitcast(BF16)
    v0row = arena[:, 3 * Lq // 2:2 * Lq].bitcast(BF16)
    v1row = arena[:, 2 * Lq:5 * Lq // 2].bitcast(BF16)
    gsl = arena[0:64, 5 * Lq // 2:5 * Lq // 2 + NCH * 128 // 2].bitcast(BF16)
    ogb = arena[0:64, 5 * Lq // 2 + NCH * 128 // 2:5 * Lq // 2 + NCH * 128].bitcast(BF16)

    pALL = ps("pALL", [128, 8 * 512], F32)
    def bank(k):
        return pALL[:, k * 512:(k + 1) * 512]
    def bankb(k):
        return pALL[:, k * 512:k * 512 + 64].bitcast(BF16)

    s_w = sem("s_w")
    s_x = sem("s_x", 2)
    s_ln = sem("s_ln")
    s_lnd = sem("s_lnd", 2)
    s_lnT = sem("s_lnT", 8)
    s_wc = sem("s_wc", 2)
    s_pj = sem("s_pj")
    s_pv = sem("s_pv")
    s_psd = sem("s_psd", 4)
    s_bal = sem("s_bal", 2)
    s_bt = sem("s_bt")
    s_g1 = sem("s_g1")
    s_g2 = sem("s_g2")
    s_phA = sem("s_phA")
    s_cvl = sem("s_cvl", 2)
    s_cv = sem("s_cv")
    s_ctn = sem("s_ctn")
    s_cvs = sem("s_cvs", 2)
    s_qkv = sem("s_qkv")
    s_oge = sem("s_oge")
    s_ogs = sem("s_ogs")
    s_tk = sem("s_tk")
    s_dbg = sem("s_dbg")

    sched = []

    def emit_chunks():
        for bh in range(4):
            for ch in range(NCH):
                t0 = ch * 64
                tile = t0 // 128
                pb = t0 % 128
                lo = (pb == 0)
                first = (ch == 0)

                def v0(e, t0=t0, bh=bh, first=first):
                    if first:
                        e.wait_ge(s_qkv, 16 * 6 * (bh + 1))
                        if bh > 0:
                            e.wait_ge(s_ogs, 16 * bh)
                        e.memset(S0f[:], 0.0)
                        e.memset(S0[:], 0.0)
                    e.tensor_copy(kTr[:].rearrange("p (t j) -> p t j", j=2)[:, :, 0:1],
                                  k0row[:, t0:t0 + 64].rearrange("p (t o) -> p t o", o=1))
                    e.tensor_copy(kTr[:].rearrange("p (t j) -> p t j", j=2)[:, :, 1:2],
                                  k1row[:, t0:t0 + 64].rearrange("p (t o) -> p t o", o=1))
                    e.tensor_copy(vTr[:].rearrange("p (t j) -> p t j", j=2)[:, :, 0:1],
                                  v0row[:, t0:t0 + 64].rearrange("p (t o) -> p t o", o=1))
                    return e.tensor_copy(vTr[:].rearrange("p (t j) -> p t j", j=2)[:, :, 1:2],
                                         v1row[:, t0:t0 + 64].rearrange("p (t o) -> p t o", o=1))
                sched.append(('v', v0, bh))

                def t1(e, t0=t0, tile=tile, bh=bh, lo=lo):
                    e.transpose(bankb(0), kTr[:], i128[:])
                    e.transpose(bankb(1), vTr[:], i128[:])
                    e.transpose(pALL[0:64, 2 * 512:2 * 512 + 64].bitcast(BF16),
                                qrow[:, t0:t0 + 64], i128[:])
                    bemat = bel if lo else beh
                    e.matmul(pALL[:, 3 * 512:3 * 512 + 1], bemat[:, 0:128],
                             barr[:, tile * 8 + bh:tile * 8 + bh + 1], start=True, stop=False)
                    e.matmul(pALL[:, 3 * 512:3 * 512 + 1], bemat[:, 128:256],
                             barr[:, tile * 8 + 4 + bh:tile * 8 + 4 + bh + 1], start=False, stop=True)
                    cemat = cel if lo else ceh
                    return e.matmul(pALL[:, 3 * 512 + 64:3 * 512 + 65], cemat[:],
                                    garr[:, tile * 4 + bh:tile * 4 + bh + 1], start=True, stop=True)
                sched.append(('t', t1, bh))

                def v1(e):
                    e.tensor_copy(Ktm[:], bankb(0))
                    e.tensor_copy(Vtm[:], bankb(1))
                    e.tensor_copy(Qtm[:], pALL[0:64, 2 * 512:2 * 512 + 64].bitcast(BF16))
                    e.tensor_copy(bcolf[:], pALL[:, 3 * 512:3 * 512 + 1])
                    e.tensor_copy(lcol[:], pALL[:, 3 * 512 + 64:3 * 512 + 65])
                    e.tensor_scalar(nlcol[:], lcol[:], -1.0, None, OP.mult)
                    e.tensor_copy(lc16[:], lcol[:])
                    e.tensor_scalar(nb16[:], bcolf[:], -1.0, None, OP.mult)
                    e.tensor_tensor(sqf[:], Ktm[:], Ktm[:], OP.mult)
                    e.tensor_reduce(rsc[:], sqf[:], mybir.AxisListType.X, OP.add)
                    e.tensor_scalar(rsc[:], rsc[:], 1e-24, None, OP.max)
                    emit_rsqrt(e, rst2[:], rsc[:], tmp1[:])
                    e.tensor_scalar(Ktm[:], Ktm[:], rst2[:], None, OP.mult)
                    e.tensor_tensor(sqf[0:64, :], Qtm[:], Qtm[:], OP.mult)
                    e.tensor_reduce(rsc[0:64, :], sqf[0:64, :], mybir.AxisListType.X, OP.add)
                    e.tensor_scalar(rsc[0:64, :], rsc[0:64, :], 1e-24, None, OP.max)
                    emit_rsqrt(e, rst2[0:64, :], rsc[0:64, :], tmp1[0:64, :])
                    e.tensor_scalar(rst2[0:64, :], rst2[0:64, :], SQD, None, OP.mult)
                    return e.tensor_scalar(Qtm16[:], Qtm[:], rst2[0:64, :], None, OP.mult)
                sched.append(('v', v1, bh))

                def _gd1_placeholder():
                    pass
                if dbg and bh == 0:
                    def gd1(e, ch=ch):
                        base = gd1.counter[0]
                        seq = [
                            (dbgd.ap()[ch, 0:128, 0:64].bitcast(BF16), Ktm[:]),
                            (dbgd.ap()[ch, 320:448, 0:64].bitcast(BF16), Vtm[:]),
                            (dbgd.ap()[ch, 256:320, :], Qtm[:]),
                            (dbgd.ap()[ch, 384:385, :].rearrange("o p -> p o"), lcol[:]),
                            (dbgd.ap()[ch, 385:386, :].rearrange("o p -> p o"), bcolf[:]),
                            (dbgd.ap()[ch, 386:387, :].rearrange("o p -> p o"), rst2[:]),
                            (dbgd.ap()[ch, 387:388, :].rearrange("o p -> p o"), acol[:]),
                            (dbgd.ap()[ch, 128:256, :], uf[:]),
                            (dbgd.ap()[ch, 640:768, 0:64].bitcast(BF16), At[:]),
                            (dbgd.ap()[ch, 448:576, :], sqf[:]),
                        ]
                        r = None
                        for kk2, (o_, i_) in enumerate(seq):
                            e.wait_ge(s_dbg, 16 * (base + kk2))
                            r = e.dma_start(out=o_, in_=i_).then_inc(s_dbg, 16)
                        gd1.counter[0] += len(seq)
                        return r
                    gd1.counter = [0]
                    _gd1_fn = gd1

                def t2(e):
                    e.transpose(bankb(0), Ktm[:], i128[:])
                    e.transpose(pALL[:, 1 * 512:1 * 512 + 32].bitcast(BF16), Qtm16[:],
                                i128[0:64, 0:64])
                    e.transpose(pALL[0:1, 2 * 512:2 * 512 + 64].bitcast(BF16), lc16[:], i128[:])
                    return e.transpose(pALL[0:1, 2 * 512 + 64:2 * 512 + 128].bitcast(BF16),
                                       nb16[:], i128[:])
                sched.append(('t', t2, bh))

                def v2(e):
                    e.tensor_copy(kT[:], bankb(0))
                    e.tensor_copy(qT[:, 0:64], pALL[:, 1 * 512:1 * 512 + 32].bitcast(BF16))
                    e.tensor_copy(lrow[:], pALL[0:1, 2 * 512:2 * 512 + 64].bitcast(BF16))
                    return e.tensor_copy(brr[:], pALL[0:1, 2 * 512 + 64:2 * 512 + 128].bitcast(BF16))
                sched.append(('v', v2, bh))

                def t3(e):
                    e.matmul(pALL[:, 0 * 512:0 * 512 + 128], kT[:], kT[:], start=True, stop=True)
                    e.matmul(pALL[:, 1 * 512:1 * 512 + 64], kT[:], qT[:, 0:64], start=True, stop=True)
                    e.matmul(pALL[:, 2 * 512:2 * 512 + 128], kT[:], S0[:], start=True, stop=True)
                    e.matmul(pALL[:, 3 * 512:3 * 512 + 128], onesr[:], lrow[:], start=True, stop=True)
                    lodd = lrow[:].rearrange("o (t j) -> o t j", j=2)[:, :, 1:2].rearrange("o t j -> o (t j)")
                    e.matmul(pALL[:, 4 * 512:4 * 512 + 64], onesr[:], lodd, start=True, stop=True)
                    e.matmul(pALL[:, 5 * 512:5 * 512 + 1], onesr[:], lrow[:, 127:128],
                             start=True, stop=True)
                    e.matmul(pALL[:, 6 * 512:6 * 512 + 128], onesr[:], brr[:], start=True, stop=True)
                    return e.matmul(pALL[0:64, 5 * 512 + 64:5 * 512 + 65], oddsel[:], lc16[:],
                                    start=True, stop=True)
                sched.append(('t', t3, bh))

                def v3(e):
                    e.tensor_tensor(ldmm[:], pALL[:, 3 * 512:3 * 512 + 128], mstr[:], OP.add)
                    e.tensor_tensor(dtq[:], pALL[:, 4 * 512:4 * 512 + 64], m2t[:], OP.add)
                    return e.tensor_tensor(kpe[:], pALL[:, 5 * 512:5 * 512 + 1], nlcol[:], OP.add)
                sched.append(('v', v3, bh))

                def a1(e):
                    e.activation(dtm[:], ldmm[:], AF.Exp, bias=nlcol[:])
                    e.activation(dtq[:], dtq[:], AF.Exp, bias=nlcol[:])
                    e.activation(acol[:], lcol[:], AF.Exp)
                    e.activation(kpe[:], kpe[:], AF.Exp)
                    e.activation(aCc[:], pALL[:, 5 * 512:5 * 512 + 1], AF.Exp)
                    return e.activation(aoc[:], pALL[0:64, 5 * 512 + 64:5 * 512 + 65], AF.Exp)
                sched.append(('a', a1, bh))

                def v4(e):
                    e.tensor_tensor(sqf[:], dtm[:], pALL[:, 0 * 512:0 * 512 + 128], OP.mult)
                    e.tensor_tensor(sqf[:], sqf[:], pALL[:, 6 * 512:6 * 512 + 128], OP.mult)
                    e.tensor_copy(At[:], sqf[:])
                    e.tensor_copy(Pt[:], At[:])
                    e.tensor_tensor(dtq[:], dtq[:], pALL[:, 1 * 512:1 * 512 + 64], OP.mult)
                    e.tensor_copy(attn[:], dtq[:])
                    e.tensor_scalar(sqf[:], pALL[:, 2 * 512:2 * 512 + 128], acol[:], None, OP.mult)
                    e.tensor_tensor(sqf[:], Vtm[:], sqf[:], OP.subtract)
                    e.tensor_scalar(uf[:], sqf[:], bcolf[:], None, OP.mult)
                    return e.tensor_scalar(Kp16[:], Ktm[:], kpe[:], None, OP.mult)
                sched.append(('v', v4, bh))

                def t4(e):
                    return e.transpose(pALL[:, 7 * 512:7 * 512 + 128], At[:], if32[:])
                sched.append(('t', t4, bh))

                def v4b(e):
                    return e.tensor_copy(Pm[:], pALL[:, 7 * 512:7 * 512 + 128])
                sched.append(('v', v4b, bh))

                for lev in range(7):
                    def tl(e, lev=lev):
                        if lev < 6:
                            e.matmul(pALL[:, 1 * 512:1 * 512 + 128], Pt[:], Pm[:], start=True, stop=True)
                            e.matmul(pALL[:, 2 * 512:2 * 512 + 128], Pm[:], Pt[:], start=True, stop=True)
                        return e.matmul(pALL[:, 0 * 512:0 * 512 + 128], Pt[:], uf[:],
                                        start=True, stop=True)
                    sched.append(('t', tl, bh))

                    def vl(e, lev=lev):
                        r = e.tensor_tensor(uf[:], uf[:], pALL[:, 0 * 512:0 * 512 + 128], OP.add)
                        if lev < 6:
                            e.tensor_copy(Pm[:], pALL[:, 1 * 512:1 * 512 + 128])
                            r = e.tensor_copy(Pt[:], pALL[:, 2 * 512:2 * 512 + 128])
                        else:
                            r = e.tensor_copy(u16[:], uf[:])
                        return r
                    sched.append(('v', vl, bh))

                if dbg and bh == 0:
                    sched.append(('g', gd1, bh))

                def t5(e):
                    e.matmul(pALL[0:64, 0 * 512:0 * 512 + 128], attn[:], u16[:], start=True, stop=True)
                    e.matmul(pALL[0:64, 1 * 512:1 * 512 + 128], qT[:, 0:64], S0[:], start=True, stop=True)
                    return e.matmul(pALL[:, 2 * 512:2 * 512 + 128], Kp16[:], u16[:],
                                    start=True, stop=True)
                sched.append(('t', t5, bh))

                def v6(e, tile=tile, pb=pb, bh=bh, ch=ch, last=(ch == NCH - 1)):
                    e.tensor_scalar(ot2[:], pALL[0:64, 1 * 512:1 * 512 + 128], aoc[:], None, OP.mult)
                    e.tensor_tensor(otile[:], pALL[0:64, 0 * 512:0 * 512 + 128], ot2[:], OP.add)
                    e.tensor_scalar(S0f[:], S0f[:], aCc[:], None, OP.mult)
                    e.tensor_tensor(S0f[:], S0f[:], pALL[:, 2 * 512:2 * 512 + 128], OP.add)
                    e.tensor_copy(S0[:], S0f[:])
                    e.tensor_tensor(ot2[:], otile[:], otile[:], OP.mult)
                    e.tensor_reduce(osc[:], ot2[:], mybir.AxisListType.X, OP.add)
                    e.tensor_scalar(osc[:], osc[:], 1.0 / 128.0, 1e-5, OP.mult, OP.add)
                    emit_rsqrt(e, rst2[0:64, :], osc[:], tmp1[0:64, :])
                    e.tensor_scalar(otile[:], otile[:], rst2[0:64, :], None, OP.mult)
                    e.tensor_tensor(otile[:], otile[:], nwm[0:64, :], OP.mult)
                    return e.tensor_tensor(ogb[:, ch * 128:(ch + 1) * 128],
                                           otile[:], gsl[:, ch * 128:(ch + 1) * 128],
                                           OP.mult)
                sched.append(('v', v6, bh))


    emit_chunks()
    NTK = len(sched)

    block = st.enter_context(nc.Block())
    if True:
        @block.sync
        def _(e):
            for i in range(NT):
                if i >= 2:
                    e.wait_ge(s_ln, i - 1)
                e.dma_start(out=xt[:, (i % 2) * D:(i % 2 + 1) * D],
                            in_=x2b[i * 128:(i + 1) * 128, :]).then_inc(s_x[i % 2], 16)
            for i in range(NT):
                th_i, si_i = i // (NT // 2), i % (NT // 2)
                idx = (th_i * NP5 + (NP5 - 1)) * (NT // 2) + si_i
                e.wait_ge(s_psd[idx % 4], 16 * (idx // 4 + 1))
                e.dma_start(out=bat[:, i * 12:(i + 1) * 12],
                            in_=projd.ap()[i * 128:(i + 1) * 128, 3072:3084]).then_inc(s_bal[i % 2], 16)
            for cc in range(NCC):
                if cc == 0:
                    e.wait_ge(s_phA, 1)
                    for q in range(4):
                        e.wait_ge(s_psd[q], 16 * (NP5 * NT // 4))
                if cc >= 1:
                    e.wait_ge(s_cv, 2 * cc)
                for tb in range(NTB):
                    e.dma_start(out=cvin[:, 128 + tb * 512:128 + (tb + 1) * 512],
                                in_=projd.ap()[tb * 512:(tb + 1) * 512, cc * 128:(cc + 1) * 128],
                                transpose=True).then_inc(s_cvl[cc % 2], 16)
            for bh in range(4):
                e.wait_ge(s_tk, [n for n, it in enumerate(sched) if it[2] == bh][-1] + 1)
                e.dma_start(out=og_d[:, bh * 128:(bh + 1) * 128].rearrange("(i p) c -> p i c", p=64),
                            in_=ogb[:].rearrange("p (i c) -> p i c", c=128)).then_inc(s_ogs, 16)
            e.wait_ge(s_ogs, 64)

        @block.gpsimd
        def _(e):
            e.dma_start(out=cw[:].rearrange("p (c f) -> p c f", c=NCC),
                        in_=cw_d.rearrange("(c p) f -> p c f", p=128)).then_inc(s_w, 16)
            consts = ((alog, alog_d), (dtb, dtb_d), (nwm, nwm_d), (cel, cel_d),
                      (ceh, ceh_d), (bel, bel_d), (beh, beh_d), (oddsel, odd_d),
                      (m2t, m2t_d), (mstr, mstr_d), (i128, id_d), (if32, id_d))
            for t, d_ in consts:
                e.dma_start(out=t[:], in_=d_[:, :]).then_inc(s_w, 16)
            NW = 1 + len(consts)
            for i in range(NT):
                e.wait_ge(s_ln, i + 1)
                e.dma_start(out=ln2d.ap()[i * 128:(i + 1) * 128, :],
                            in_=ln2t[:, (i % 2) * D:(i % 2 + 1) * D]).then_inc(s_lnd[i % 2], 16)
            for q in range(2 * NP5 + 1):
                if q <= 2 * NP5 - 1:
                    i5 = q % NP5
                    if q >= 2:
                        e.wait_ge(s_pj, (q - 1) * (NT // 2))
                    for c in range(NKC):
                        e.dma_start(out=wcols[:, (q % 2) * 16 * 512 + c * 512:
                                              (q % 2) * 16 * 512 + (c + 1) * 512],
                                    in_=Wall_d[c * 128:(c + 1) * 128, i5 * 512:(i5 + 1) * 512]
                                    ).then_inc(s_wc[q % 2], 16)
                if q >= 1:
                    qp = q - 1
                    thp, i5p = qp // NP5, qp % NP5
                    for si in range(NT // 2):
                        idx = qp * (NT // 2) + si
                        s = thp * (NT // 2) + si
                        e.wait_ge(s_pv, idx + 1)
                        e.dma_start(out=projd.ap()[s * 128:(s + 1) * 128, i5p * 512:(i5p + 1) * 512],
                                    in_=ebuf[:, (idx % 4) * 512:(idx % 4 + 1) * 512]
                                    ).then_inc(s_psd[idx % 4], 16)
            for cc in range(NCC):
                e.wait_ge(s_cv, 2 * (cc + 1))
                e.dma_start(out=convd.ap()[cc * 128:(cc + 1) * 128, :],
                            in_=cvout[:, 128:128 + Lq]).then_inc(s_cvs[cc % 2], 16)
            for bh in range(4):
                if bh == 0:
                    e.wait_ge(s_cvs[0], 16 * ((NCC + 1) // 2))
                    e.wait_ge(s_cvs[1], 16 * (NCC // 2))
                else:
                    e.wait_ge(s_tk, [n for n, it in enumerate(sched) if it[2] == bh][0])
                h = bh
                e.dma_start(out=qrow[:, :], in_=convd.ap()[h * 128:(h + 1) * 128, :]).then_inc(s_qkv, 16)
                e.dma_start(out=k0row[:, :], in_=convd.ap()[512 + h * 128:512 + (h + 1) * 128, :]).then_inc(s_qkv, 16)
                e.dma_start(out=k1row[:, :], in_=convd.ap()[1024 + h * 128:1024 + (h + 1) * 128, :]).then_inc(s_qkv, 16)
                e.dma_start(out=v0row[:, :], in_=convd.ap()[1536 + h * 128:1536 + (h + 1) * 128, :]).then_inc(s_qkv, 16)
                e.dma_start(out=v1row[:, :], in_=convd.ap()[2048 + h * 128:2048 + (h + 1) * 128, :]).then_inc(s_qkv, 16)
                e.dma_start(out=gsl[:].rearrange("p (i c) -> p i c", c=128),
                            in_=projd.ap()[:, 2560 + h * 128:2560 + (h + 1) * 128].rearrange(
                                "(i p) c -> p i c", p=64)).then_inc(s_qkv, 16)
                if dbg:
                    gcnt = 0
                    for n, (eng, fn, bhs) in enumerate(sched):
                        if eng == 'g':
                            gcnt += 1
                            if bhs == bh:
                                e.wait_ge(s_tk, n)
                                fn(e)
                                e.wait_ge(s_dbg, 160 * gcnt)
                                e.nop().then_inc(s_tk, 1)

        @block.vector
        def _(e):
            e = AutoDrain(e)
            e.memset(onesr[:], 1.0)
            e.wait_ge(s_w, 16 * 13)
            e.tensor_scalar(cw[:], cw[:], 0.5, None, OP.mult)
            for i in range(NT):
                e.wait_ge(s_x[i % 2], 16 * (i // 2 + 1))
                if i >= 2:
                    e.wait_ge(s_lnd[i % 2], 16 * (i // 2))
                xs = xt[:, (i % 2) * D:(i % 2 + 1) * D]
                emit_ln_stats(e, bst, mv, nmu, veps, rstd, tmp1, xs)
                e.tensor_scalar(ln2t[:, (i % 2) * D:(i % 2 + 1) * D], xs,
                                nmu[:], rstd[:], OP.add, OP.mult).then_inc(s_ln, 1)
            for i in range(NT):
                e.wait_ge(s_bt, i + 1)
                e.tensor_scalar(barr[:, i * 8:(i + 1) * 8], bat[:, i * 12:i * 12 + 8],
                                1.0, None, OP.add)
                e.tensor_tensor(zt3[:, i * 4:(i + 1) * 4], bat[:, i * 12 + 8:i * 12 + 12],
                                dtb[:], OP.add).then_inc(s_g1, 1)
            for cc in range(NCC):
                e.wait_ge(s_cvl[cc % 2], 16 * NTB * (cc // 2 + 1))
                if cc >= 1:
                    e.wait_ge(s_cvs[(cc - 1) % 2], 16 * ((cc - 1) // 2 + 1))
                e.memset(cvin[:, 125:128], 0.0)
                w0 = cw[:, cc * 4 + 0:cc * 4 + 1]
                w1 = cw[:, cc * 4 + 1:cc * 4 + 2]
                w2 = cw[:, cc * 4 + 2:cc * 4 + 3]
                w3 = cw[:, cc * 4 + 3:cc * 4 + 4]
                co = cvout[:, 128:128 + Lq]
                e.tensor_scalar(co, cvin[:, 125:125 + Lq], w0, None, OP.mult)
                e.scalar_tensor_tensor(co, cvin[:, 126:126 + Lq], w1, co, OP.mult, OP.add)
                e.scalar_tensor_tensor(co, cvin[:, 127:127 + Lq], w2, co, OP.mult, OP.add)
                e.scalar_tensor_tensor(co, cvin[:, 128:128 + Lq], w3, co,
                                       OP.mult, OP.add).then_inc(s_cv, 1)
                e.wait_ge(s_ctn, cc + 1)
                e.scalar_tensor_tensor(co, cvin[:, 128:128 + Lq], 1.0, co,
                                       OP.add, OP.mult).then_inc(s_cv, 1)
            for i in range(NT):
                e.wait_ge(s_g2, i + 1)
                gg = garr[:, i * 4:(i + 1) * 4]
                e.tensor_scalar(gg, zt3[:, i * 4:(i + 1) * 4], 0.0, None, OP.max)
                e.tensor_tensor(gg, gg, zt4[:, i * 4:(i + 1) * 4], OP.add)
                e.tensor_tensor(gg, gg, negea[:], OP.mult)
            for n, (eng, fn, bhs) in enumerate(sched):
                if eng == 'v':
                    e.wait_ge(s_tk, n)
                    fn(e).then_inc(s_tk, 1)

        @block.scalar
        def _(e):
            e = AutoDrain(e)
            for th in range(2):
                if th == 1:
                    e.wait_ge(s_pj, NP5 * (NT // 2))
                for tb in range(NTB // 2):
                    j = th * (NTB // 2) + tb
                    e.wait_ge(s_lnd[0], 16 * 2 * (j + 1))
                    e.wait_ge(s_lnd[1], 16 * 2 * (j + 1))
                    for c in range(NKC):
                        e.dma_start(out=lnT[:, c * (Lq // 2) + tb * 512:c * (Lq // 2) + (tb + 1) * 512],
                                    in_=ln2d.ap()[j * 512:(j + 1) * 512, c * 128:(c + 1) * 128],
                                    transpose=True).then_inc(s_lnT[th], 16)
                for i5 in range(NP5):
                    q = th * NP5 + i5
                    for si in range(NT // 2):
                        idx = q * (NT // 2) + si
                        e.wait_ge(s_pj, idx + 1)
                        if idx >= 4:
                            e.wait_ge(s_psd[idx % 4], 16 * (idx // 4))
                        func = AF.Silu if i5 == 5 else AF.Copy
                        e.activation(ebuf[:, (idx % 4) * 512:(idx % 4 + 1) * 512],
                                     bank(idx % 4), func).then_inc(s_pv, 1)
            e.wait_ge(s_bal[0], 16 * ((NT + 1) // 2))
            e.wait_ge(s_bal[1], 16 * (NT // 2))
            for i in range(NT):
                e.activation(bat[:, i * 12:i * 12 + 8], bat[:, i * 12:i * 12 + 8],
                             AF.Tanh, scale=0.5).then_inc(s_bt, 1)
            for cc in range(NCC):
                e.wait_ge(s_cv, 2 * cc + 1)
                e.activation(cvin[:, 128:128 + Lq], cvout[:, 128:128 + Lq],
                             AF.Tanh).then_inc(s_ctn, 1)
            e.activation(negea[:], alog[:], AF.Exp)
            e.activation(negea[:], negea[:], AF.Copy, scale=-1.0)
            for i in range(NT):
                e.wait_ge(s_g1, i + 1)
                e.activation(zt4[:, i * 4:(i + 1) * 4], zt3[:, i * 4:(i + 1) * 4], AF.Abs)
                e.activation(zt4[:, i * 4:(i + 1) * 4], zt4[:, i * 4:(i + 1) * 4], AF.Exp, scale=-1.0)
                e.activation(zt4[:, i * 4:(i + 1) * 4], zt4[:, i * 4:(i + 1) * 4], AF.Ln,
                             bias=1.0).then_inc(s_g2, 1)
            for n, (eng, fn, bhs) in enumerate(sched):
                if eng == 'a':
                    e.wait_ge(s_tk, n)
                    fn(e).then_inc(s_tk, 1)

        @block.tensor
        def _(e):
            e.wait_ge(s_w, 16 * 13)
            for th in range(2):
                e.wait_ge(s_lnT[th], 16 * 16 * (NTB // 2))
                for i5 in range(NP5):
                    q = th * NP5 + i5
                    e.wait_ge(s_wc[q % 2], 16 * 16 * (q // 2 + 1))
                    for si in range(NT // 2):
                        idx = q * (NT // 2) + si
                        if idx >= 4:
                            e.wait_ge(s_pv, idx - 3)
                        for c in range(NKC):
                            mm = e.matmul(bank(idx % 4),
                                          lnT[:, c * (Lq // 2) + si * 128:c * (Lq // 2) + (si + 1) * 128],
                                          wcols[:, (q % 2) * 16 * 512 + c * 512:(q % 2) * 16 * 512 + (c + 1) * 512],
                                          start=(c == 0), stop=(c == NKC - 1))
                        mm.then_inc(s_pj, 1)
            e.nop().then_inc(s_phA, 1)
            lastbh = -1
            for n, (eng, fn, bhs) in enumerate(sched):
                if eng == 't':
                    if bhs != lastbh:
                        e.wait_ge(s_qkv, 16 * 6 * (bhs + 1))
                        lastbh = bhs
                    e.wait_ge(s_tk, n)
                    fn(e).then_inc(s_tk, 1)
    st.close()
    return nc



QH = 5632

def build_l1ffn(Tp=CTOK):
    nc = bass.Bass("TRN2", target_bir_lowering=False)
    x2o = nc.dram_tensor("x2o", [Tp, D], F32, kind="ExternalInput").ap()
    ogf = nc.dram_tensor("ogf", [Tp, 1024], BF16, kind="ExternalInput").ap()
    Wo1_d = nc.dram_tensor("Wo1", [1024, D], F32, kind="ExternalInput").ap()
    wqg_d = nc.dram_tensor("wqgT", [D, QH], I32, kind="ExternalInput").ap()
    wqu_d = nc.dram_tensor("wquT", [D, QH], I32, kind="ExternalInput").ap()
    wqd_d = nc.dram_tensor("wqdT", [QH, D], I32, kind="ExternalInput").ap()
    sgm_d = nc.dram_tensor("sgm", [128, QH], F32, kind="ExternalInput").ap()
    sum_d = nc.dram_tensor("sum_", [128, QH], F32, kind="ExternalInput").ap()
    sdm_d = nc.dram_tensor("sdm", [128, D], F32, kind="ExternalInput").ap()
    x4 = nc.dram_tensor("x4", [Tp, D], F32, kind="ExternalOutput").ap()
    x3d = nc.dram_tensor("x3d", [Tp, D], F32)
    ln3d = nc.dram_tensor("ln3d", [Tp, D], BF16)
    zd = nc.dram_tensor("zd", [Tp, QH], BF16)

    NT = Tp // 128
    NTB = Tp // 512
    NKC = D // 128
    NH5 = QH // 512
    NHC = QH // 128
    NTS_G = min(8, NT)
    NTG = NT // NTS_G

    from contextlib import ExitStack
    st = ExitStack()
    def sb(*a, **k):
        return st.enter_context(nc.sbuf_tensor(*a, **k))
    def ps(*a, **k):
        return st.enter_context(nc.psum_tensor(*a, **k))
    def sem(n, k=1):
        if k == 1:
            return st.enter_context(nc.semaphore(n))
        return [st.enter_context(nc.semaphore(f"{n}{i}")) for i in range(k)]

    xt = sb("xt", [128, 2 * D], F32)
    bst = sb("bst", [128, 24], F32)
    mv = sb("mv", [128, 2], F32)
    nmu = sb("nmu", [128, 1], F32)
    veps = sb("veps", [128, 1], F32)
    rstd = sb("rstd", [128, 1], F32)
    tmp1 = sb("tmp1", [128, 1], F32)
    ln3t = sb("ln3t", [128, 2 * D], BF16)
    x3t = sb("x3t", [128, 2 * D], F32)
    xs = sb("xs", [128, NT], F32)
    xsn = sb("xsn", [128, NT], F32)
    zs = sb("zs", [128, NT], F32)
    zsn = sb("zsn", [128, NT], F32)
    zred = sb("zred", [128, 1], F32)
    sgm = sb("sgm_s", [128, QH], BF16)
    summ = sb("sum_s", [128, QH], BF16)
    sdm = sb("sdm_s", [128, D], BF16)
    g1 = sb("g1", [128, 512], F32)
    sg = sb("sg", [128, 512], BF16)
    u1 = sb("u1", [128, 512], F32)
    zt = sb("zt", [128, 2 * 512], BF16)
    AW_A = 8 * Tp // 2 + 8 * D // 2
    AW_LNT = 16 * (Tp // 2) // 2
    AW_B = AW_LNT + 2 * 2 * 16 * 512 // 2
    AW_C = NTS_G * D + 256 + 512 + 1024 + 2 * D
    ar_words = max(AW_A, AW_B, AW_C)
    arena = sb("arena", [128, ar_words], F32, is_transpose_dest=True)
    ogT = arena[:, 0:8 * Tp // 2].bitcast(BF16)
    wo1 = arena[:, 8 * Tp // 2:AW_A].bitcast(BF16)
    lnT = arena[:, 0:AW_LNT].bitcast(BF16)
    wqc = arena[:, AW_LNT:AW_B].bitcast(BF16)
    otile = arena[:, 0:NTS_G * D]
    bdt = arena[:, NTS_G * D:NTS_G * D + 512].bitcast(BF16)
    hsTt = arena[:, NTS_G * D + 512:NTS_G * D + 512 + 1024].bitcast(BF16)
    res = arena[:, NTS_G * D + 512 + 1024:NTS_G * D + 512 + 1024 + 2 * D]

    pALL = ps("pALL", [128, 8 * 512], F32)
    def bank(k):
        return pALL[:, k * 512:(k + 1) * 512]

    s_x = sem("s_x", 2)
    s_ogT = sem("s_ogT", 2)
    s_w = sem("s_w")
    s_am = sem("s_am")
    s_res1 = sem("s_res1")
    s_x3s = sem("s_x3s", 2)
    s_lns = sem("s_lns", 2)
    s_lnT = sem("s_lnT", 2)
    s_wq = sem("s_wq", 2)
    s_gu = sem("s_gu")
    s_sg = sem("s_sg")
    s_zt = sem("s_zt")
    s_zds = sem("s_zds", 2)
    s_zsum = sem("s_zsum")
    s_bd = sem("s_bd", 2)
    s_hT = sem("s_hT", 2)
    s_ct = sem("s_ct")
    s_mm = sem("s_mm")
    s_ev = sem("s_ev")
    s_x3r = sem("s_x3r", 2)
    s_res = sem("s_res")
    s_st = sem("s_st", 2)
    block = st.enter_context(nc.Block())
    if True:
        @block.sync
        def _(e):
            for j in range(NTB):
                if j >= 2:
                    e.wait_ge(s_am, 4 * (j - 1))
                for c in range(8):
                    e.dma_start(out=ogT[:, (j % 2) * 8 * 512 + c * 512:(j % 2) * 8 * 512 + (c + 1) * 512],
                                in_=ogf[j * 512:(j + 1) * 512, c * 128:(c + 1) * 128],
                                transpose=True).then_inc(s_ogT[j % 2], 16)
                for i in range(4 * j, 4 * j + 4):
                    if i >= 2:
                        e.wait_ge(s_res1, i - 1)
                    e.dma_start(out=xt[:, (i % 2) * D:(i % 2 + 1) * D],
                                in_=x2o[i * 128:(i + 1) * 128, :]).then_inc(s_x[i % 2], 16)
            for th in range(2):
                if th == 0:
                    e.wait_ge(s_am, NT)
                else:
                    e.wait_ge(s_gu, NH5 * (NT // 2))
                for tb in range(NTB // 2):
                    jj = th * (NTB // 2) + tb
                    e.wait_ge(s_lns[0], 16 * 2 * (jj + 1))
                    e.wait_ge(s_lns[1], 16 * 2 * (jj + 1))
                    for c in range(NKC):
                        e.dma_start(out=lnT[:, c * (Tp // 2) + tb * 512:c * (Tp // 2) + (tb + 1) * 512],
                                    in_=ln3d.ap()[jj * 512:(jj + 1) * 512, c * 128:(c + 1) * 128],
                                    transpose=True).then_inc(s_lnT[th], 16)
            for tg in range(NTG):
                for d4 in range(4):
                    g = tg * 4 + d4
                    for hc in range(NHC):
                        idx = g * NHC + hc
                        if idx == 0:
                            e.wait_ge(s_zds[0], 16 * ((NH5 * NT + 1) // 2))
                            e.wait_ge(s_zds[1], 16 * (NH5 * NT // 2))
                        if idx >= 2:
                            jp = idx - 2
                            if jp % NHC == NHC - 1:
                                e.wait_ge(s_ev, jp // NHC + 1)
                            else:
                                e.wait_ge(s_ct, jp + 1 - (jp // NHC))
                        e.dma_start(out=hsTt[:, (idx % 2) * 1024:(idx % 2) * 1024 + NTS_G * 128],
                                    in_=zd.ap()[0:0] if False else
                                        zd.ap().rearrange("t h -> h t") if False else
                                        zd.ap()[tg * NTS_G * 128:(tg + 1) * NTS_G * 128,
                                                hc * 128:(hc + 1) * 128],
                                    transpose=True).then_inc(s_hT[idx % 2], 16)
                for ts_ in range(NTS_G):
                    k = tg * NTS_G + ts_
                    if k == 0:
                        e.wait_ge(s_res1, NT)
                    e.wait_ge(s_x3s[k % 2], 16 * (k // 2 + 1))
                    if k >= 2:
                        e.wait_ge(s_res, k - 1)
                    e.dma_start(out=xt[:, (k % 2) * D:(k % 2 + 1) * D],
                                in_=x3d.ap()[k * 128:(k + 1) * 128, :]).then_inc(s_x3r[k % 2], 16)
                    if k >= 2:
                        e.dma_start(out=x4[(k - 2) * 128:(k - 1) * 128, :],
                                    in_=res[:, ((k - 2) % 2) * D:((k - 2) % 2 + 1) * D]
                                    ).then_inc(s_st[(k - 2) % 2], 16)
            for k in range(NT, NT + 2):
                e.wait_ge(s_res, k - 1)
                e.dma_start(out=x4[(k - 2) * 128:(k - 1) * 128, :],
                            in_=res[:, ((k - 2) % 2) * D:((k - 2) % 2 + 1) * D]
                            ).then_inc(s_st[(k - 2) % 2], 16)
            e.wait_ge(s_st[0], 16 * ((NT + 1) // 2))
            e.wait_ge(s_st[1], 16 * (NT // 2))

        @block.gpsimd
        def _(e):
            for c in range(8):
                e.dma_start(out=wo1[:, c * D:(c + 1) * D],
                            in_=Wo1_d[c * 128:(c + 1) * 128, :]).then_inc(s_w, 16)
            e.dma_start(out=sgm[:], in_=sgm_d[:, :]).then_inc(s_w, 16)
            e.dma_start(out=summ[:], in_=sum_d[:, :]).then_inc(s_w, 16)
            e.dma_start(out=sdm[:], in_=sdm_d[:, :]).then_inc(s_w, 16)
            for i in range(NT):
                e.wait_ge(s_res1, i + 1)
                e.dma_start(out=x3d.ap()[i * 128:(i + 1) * 128, :],
                            in_=x3t[:, (i % 2) * D:(i % 2 + 1) * D]).then_inc(s_x3s[i % 2], 16)
                e.dma_start(out=ln3d.ap()[i * 128:(i + 1) * 128, :],
                            in_=ln3t[:, (i % 2) * D:(i % 2 + 1) * D]).then_inc(s_lns[i % 2], 16)
            for q5 in range(2 * NH5 + 1):
                if q5 <= 2 * NH5 - 1:
                    th, h5 = q5 // NH5, q5 % NH5
                    if q5 == 0:
                        e.wait_ge(s_am, NT)
                    if q5 >= 2:
                        e.wait_ge(s_gu, (q5 - 1) * (NT // 2))
                    for c in range(NKC):
                        e.dma_start(out=wqc[:, (q5 % 2) * 2 * 16 * 512 + c * 512:
                                            (q5 % 2) * 2 * 16 * 512 + (c + 1) * 512],
                                    in_=wqg_d[c * 128:(c + 1) * 128, h5 * 512:(h5 + 1) * 512]
                                    ).then_inc(s_wq[q5 % 2], 16)
                        e.dma_start(out=wqc[:, (q5 % 2) * 2 * 16 * 512 + 16 * 512 + c * 512:
                                            (q5 % 2) * 2 * 16 * 512 + 16 * 512 + (c + 1) * 512],
                                    in_=wqu_d[c * 128:(c + 1) * 128, h5 * 512:(h5 + 1) * 512]
                                    ).then_inc(s_wq[q5 % 2], 16)
                if q5 >= 1:
                    q5p = q5 - 1
                    thp, h5p = q5p // NH5, q5p % NH5
                    for ti in range(NT // 2):
                        idx = q5p * (NT // 2) + ti
                        i = thp * (NT // 2) + ti
                        e.wait_ge(s_zt, idx + 1)
                        e.dma_start(out=zd.ap()[i * 128:(i + 1) * 128, h5p * 512:(h5p + 1) * 512],
                                    in_=zt[:, (idx % 2) * 512:(idx % 2 + 1) * 512]).then_inc(s_zds[idx % 2], 16)
            for tg in range(NTG):
                for d4 in range(4):
                    g2 = tg * 4 + d4
                    for hc in range(NHC):
                        idx = g2 * NHC + hc
                        if idx == 0:
                            e.wait_ge(s_zds[0], 16 * ((NH5 * NT + 1) // 2))
                            e.wait_ge(s_zds[1], 16 * (NH5 * NT // 2))
                        if idx >= 2:
                            jp = idx - 2
                            if jp % NHC == NHC - 1:
                                e.wait_ge(s_ev, jp // NHC + 1)
                            else:
                                e.wait_ge(s_ct, jp + 1 - (jp // NHC))
                        e.dma_start(out=bdt[:, (idx % 2) * 512:(idx % 2 + 1) * 512],
                                    in_=wqd_d[hc * 128:(hc + 1) * 128, d4 * 512:(d4 + 1) * 512]
                                    ).then_inc(s_bd[idx % 2], 16)

        @block.vector
        def _(e):
            e = AutoDrain(e)
            for i in range(NT):
                e.wait_ge(s_am, i + 1)
                e.wait_ge(s_x[i % 2], 16 * (i // 2 + 1))
                if i >= 2:
                    e.wait_ge(s_x3s[i % 2], 16 * (i // 2))
                    e.wait_ge(s_lns[i % 2], 16 * (i // 2))
                pb = (i % 2) * 4
                emit_ln_stats(e, bst, mv, nmu, veps, rstd, tmp1, pALL[:, pb * 512:(pb + 4) * 512])
                x3s = x3t[:, (i % 2) * D:(i % 2 + 1) * D]
                e.tensor_scalar(x3s, pALL[:, pb * 512:(pb + 4) * 512], nmu[:], rstd[:],
                                OP.add, OP.mult)
                e.tensor_tensor(x3s, x3s, xt[:, (i % 2) * D:(i % 2 + 1) * D], OP.add)
                emit_ln_stats(e, bst, mv, nmu, veps, rstd, tmp1, x3s)
                e.tensor_scalar(ln3t[:, (i % 2) * D:(i % 2 + 1) * D], x3s,
                                nmu[:], rstd[:], OP.add, OP.mult)
                e.tensor_reduce(xs[:, i:i + 1], ln3t[:, (i % 2) * D:(i % 2 + 1) * D],
                                mybir.AxisListType.X, OP.add)
                e.tensor_scalar(xsn[:, i:i + 1], xs[:, i:i + 1], -128.0, None,
                                OP.mult).then_inc(s_res1, 1)
            for th in range(2):
                for h5 in range(NH5):
                    for ti in range(NT // 2):
                        idx = (th * NH5 + h5) * (NT // 2) + ti
                        i = th * (NT // 2) + ti
                        e.wait_ge(s_gu, idx + 1)
                        gb = bank(4 + 2 * (idx % 2))
                        ub = bank(5 + 2 * (idx % 2))
                        e.tensor_scalar(g1[:], gb, xsn[:, i:i + 1], None, OP.add)
                        e.tensor_tensor(g1[:], g1[:], sgm[:, h5 * 512:(h5 + 1) * 512],
                                        OP.mult).then_inc(s_sg, 1)
                        e.tensor_scalar(u1[:], ub, xsn[:, i:i + 1], None, OP.add)
                        e.tensor_tensor(u1[:], u1[:], summ[:, h5 * 512:(h5 + 1) * 512], OP.mult)
                        e.wait_ge(s_sg, 2 * idx + 2)
                        if idx >= 2:
                            e.wait_ge(s_zds[idx % 2], 16 * (idx // 2))
                        e.tensor_tensor(zt[:, (idx % 2) * 512:(idx % 2 + 1) * 512], sg[:], u1[:],
                                        OP.mult).then_inc(s_zt, 1)
                        e.tensor_reduce(zred[:], zt[:, (idx % 2) * 512:(idx % 2 + 1) * 512],
                                        mybir.AxisListType.X, OP.add)
                        if h5 == 0:
                            e.tensor_copy(zs[:, i:i + 1], zred[:])
                        else:
                            e.tensor_tensor(zs[:, i:i + 1], zs[:, i:i + 1], zred[:], OP.add)
                        if h5 == NH5 - 1:
                            e.tensor_scalar(zsn[:, i:i + 1], zs[:, i:i + 1], -128.0, None, OP.mult)
            for k in range(NT):
                tg, ts_ = k // NTS_G, k % NTS_G
                e.wait_ge(s_ev, tg * 4 + 4)
                e.wait_ge(s_x3r[k % 2], 16 * (k // 2 + 1))
                if k >= 2:
                    e.wait_ge(s_st[k % 2], 16 * (k // 2))
                osl = otile[:, ts_ * D:(ts_ + 1) * D]
                e.tensor_scalar(osl, osl, zsn[:, k:k + 1], None, OP.add)
                e.tensor_tensor(osl, osl, sdm[:], OP.mult)
                emit_ln_stats(e, bst, mv, nmu, veps, rstd, tmp1, osl)
                e.tensor_scalar(res[:, (k % 2) * D:(k % 2 + 1) * D], osl,
                                nmu[:], rstd[:], OP.add, OP.mult)
                e.tensor_tensor(res[:, (k % 2) * D:(k % 2 + 1) * D],
                                res[:, (k % 2) * D:(k % 2 + 1) * D],
                                xt[:, (k % 2) * D:(k % 2 + 1) * D],
                                OP.add).then_inc(s_res, 1)

        @block.scalar
        def _(e):
            e = AutoDrain(e)
            for idx in range(NH5 * NT):
                e.wait_ge(s_sg, 2 * idx + 1)
                e.activation(sg[:], g1[:], AF.Silu).then_inc(s_sg, 1)
            for tg in range(NTG):
                for d4 in range(4):
                    g = tg * 4 + d4
                    e.wait_ge(s_mm, g + 1)
                    if d4 == 0 and tg >= 1:
                        e.wait_ge(s_res, tg * NTS_G)
                    for ts_ in range(NTS_G):
                        e.activation(otile[:, ts_ * D + d4 * 512:ts_ * D + (d4 + 1) * 512],
                                     bank(ts_), AF.Copy)
                    e.nop().then_inc(s_ev, 1)

        @block.tensor
        def _(e):
            e.wait_ge(s_w, 16 * 11)
            for i in range(NT):
                j = i // 4
                e.wait_ge(s_ogT[j % 2], 16 * 8 * (j // 2 + 1))
                if i >= 2:
                    e.wait_ge(s_res1, i - 1)
                pb = (i % 2) * 4
                for d4 in range(4):
                    for c in range(8):
                        mm = e.matmul(pALL[:, (pb + d4) * 512:(pb + d4 + 1) * 512],
                                      ogT[:, (j % 2) * 8 * 512 + c * 512 + (i % 4) * 128:
                                          (j % 2) * 8 * 512 + c * 512 + (i % 4) * 128 + 128],
                                      wo1[:, c * D + d4 * 512:c * D + (d4 + 1) * 512],
                                      start=(c == 0), stop=(c == 7))
                mm.then_inc(s_am, 1)
            e.wait_ge(s_res1, NT)
            for th in range(2):
                e.wait_ge(s_lnT[th], 16 * 16 * (NTB // 2))
                for h5 in range(NH5):
                    q5 = th * NH5 + h5
                    e.wait_ge(s_wq[q5 % 2], 16 * 32 * (q5 // 2 + 1))
                    for ti in range(NT // 2):
                        idx = q5 * (NT // 2) + ti
                        if idx >= 2:
                            e.wait_ge(s_zt, idx - 1)
                        gb = bank(4 + 2 * (idx % 2))
                        ub = bank(5 + 2 * (idx % 2))
                        for c in range(NKC):
                            e.matmul(gb, lnT[:, c * (Tp // 2) + ti * 128:c * (Tp // 2) + (ti + 1) * 128],
                                     wqc[:, (q5 % 2) * 2 * 16 * 512 + c * 512:(q5 % 2) * 2 * 16 * 512 + (c + 1) * 512],
                                     start=(c == 0), stop=(c == NKC - 1))
                        for c in range(NKC):
                            mm = e.matmul(ub, lnT[:, c * (Tp // 2) + ti * 128:c * (Tp // 2) + (ti + 1) * 128],
                                          wqc[:, (q5 % 2) * 2 * 16 * 512 + 16 * 512 + c * 512:
                                              (q5 % 2) * 2 * 16 * 512 + 16 * 512 + (c + 1) * 512],
                                          start=(c == 0), stop=(c == NKC - 1))
                        mm.then_inc(s_gu, 1)
            for tg in range(NTG):
                for d4 in range(4):
                    g = tg * 4 + d4
                    if g >= 1:
                        e.wait_ge(s_ev, g)
                    for hc in range(NHC):
                        idx = g * NHC + hc
                        e.wait_ge(s_bd[idx % 2], 16 * (idx // 2 + 1))
                        e.wait_ge(s_hT[idx % 2], 16 * (idx // 2 + 1))
                        for ts_ in range(NTS_G):
                            mm = e.matmul(bank(ts_),
                                          hsTt[:, (idx % 2) * 1024 + ts_ * 128:(idx % 2) * 1024 + (ts_ + 1) * 128],
                                          bdt[:, (idx % 2) * 512:(idx % 2 + 1) * 512],
                                          start=(hc == 0), stop=(hc == NHC - 1))
                            if ts_ == NTS_G - 1:
                                mm.then_inc(s_mm if hc == NHC - 1 else s_ct, 1)
    st.close()
    return nc



HEADCOLS = 1024

_nc_cache = {}

def _get(name, builder):
    if name not in _nc_cache:
        _nc_cache[name] = builder()
    return _nc_cache[name]


def run_l0a_heads(x, Wqk, bqk, Wv, bv):
    nc = _get("l0a_heads", build_l0a_heads)
    in_maps = []
    for c in range(8):
        b, par = c // 2, c % 2
        in_maps.append({
            "xb": np.ascontiguousarray(x[b]),
            "Wqk": np.ascontiguousarray(Wqk[:, par * 1024:(par + 1) * 1024]),
            "bqk": np.ascontiguousarray(bqk[None, par * 1024:(par + 1) * 1024]),
            "Wv": np.ascontiguousarray(Wv[:, par * 1024:(par + 1) * 1024]),
            "bv": np.ascontiguousarray(bv[None, par * 1024:(par + 1) * 1024]),
        })
    res = run_bass_kernel_spmd(nc, in_maps, core_ids=list(range(8)))
    out = np.zeros((B, L, D), np.float32)
    for c in range(8):
        b, par = c // 2, c % 2
        out[b, :, par * 1024:(par + 1) * 1024] = res.results[c]["attn_out"]
    return out


def run_l0a_epi(x, attn, Wu, bu, Wo, bo, ln_g, ln_b):
    nc = _get("l0a_epi", build_l0a_epi)
    gmap = np.ascontiguousarray(np.broadcast_to(ln_g[None, :], (128, D)))
    bmap = np.ascontiguousarray(np.broadcast_to(ln_b[None, :], (128, D)))
    in_maps = []
    for c in range(8):
        b, par = c // 2, c % 2
        sl = slice(par * CTOK, (par + 1) * CTOK)
        in_maps.append({
            "xo": np.ascontiguousarray(x[b, sl]),
            "ao": np.ascontiguousarray(attn[b, sl]),
            "Wu": np.ascontiguousarray(Wu), "bu": bu[None, :],
            "Wo": np.ascontiguousarray(Wo), "bo": bo[None, :],
            "gmap": gmap, "bmap": bmap,
        })
    res = run_bass_kernel_spmd(nc, in_maps, core_ids=list(range(8)))
    out = np.zeros((B, L, D), np.float32)
    for c in range(8):
        b, par = c // 2, c % 2
        out[b, par * CTOK:(par + 1) * CTOK] = res.results[c]["x1"]
    return out


_selmat = None

def run_l0ffn(x1, inp):
    nc = _get("l0ffn", build_l0ffn)
    global _selmat
    if _selmat is None:
        _selmat = np.zeros((8, 8 * 128), np.float32)
        for e in range(8):
            _selmat[e, e * 128:(e + 1) * 128] = 1.0
    ident = np.eye(128, dtype=np.float32)
    cWT = np.ascontiguousarray(inp['L0_ffn_conf_W'].T)
    wmap = np.ascontiguousarray(np.broadcast_to(inp['L0_ffn_expert_wealth'][None, :], (128, 8)))
    in_maps = []
    for c in range(8):
        b, par = c // 2, c % 2
        sl = slice(par * CTOK, (par + 1) * CTOK)
        in_maps.append({
            "x1o": np.ascontiguousarray(x1[b, sl]),
            "cWT": cWT, "cb16": inp['L0_ffn_conf_b'][None, :], "wmap": wmap,
            "bgate": inp['L0_ffn_base_gate'], "bup": inp['L0_ffn_base_up'],
            "bdown": inp['L0_ffn_base_down'],
            "gA": inp['L0_ffn_gate_A'], "gB": inp['L0_ffn_gate_B'],
            "uA": inp['L0_ffn_up_A'], "uB": inp['L0_ffn_up_B'],
            "dA": inp['L0_ffn_down_A'], "dB": inp['L0_ffn_down_B'],
            "selmat": _selmat, "ident": ident,
        })
    res = run_bass_kernel_spmd(nc, in_maps, core_ids=list(range(8)))
    out = np.zeros((B, L, D), np.float32)
    for c in range(8):
        b, par = c // 2, c % 2
        out[b, par * CTOK:(par + 1) * CTOK] = res.results[c]["x2"]
    return out


_l1_consts = None

def _l1attn_consts():
    global _l1_consts
    if _l1_consts is None:
        cel = np.zeros((128, 128), np.float32)
        ceh = np.zeros((128, 128), np.float32)
        for tau in range(128):
            for p in range(64):
                if p <= tau // 2:
                    cel[p, tau] = 1.0
                    ceh[64 + p, tau] = 1.0
        bel = np.zeros((128, 256), np.float32)
        beh = np.zeros((128, 256), np.float32)
        for tau in range(128):
            j = tau % 2
            t = tau // 2
            bel[t, j * 128 + tau] = 1.0
            beh[64 + t, j * 128 + tau] = 1.0
        oddsel = np.zeros((128, 64), np.float32)
        for t in range(64):
            oddsel[2 * t + 1, t] = 1.0
        m2t = np.where(np.arange(128)[:, None] <= (2 * np.arange(64)[None, :] + 1),
                       0.0, -1e30).astype(np.float32)
        mstr = np.where(np.arange(128)[:, None] < np.arange(128)[None, :],
                        0.0, -1e30).astype(np.float32)
        _l1_consts = (cel, ceh, bel, beh, oddsel, m2t, mstr)
    return _l1_consts


def run_l1attn(x2, inp):
    nc = _get("l1attn", build_l1attn)
    cel, ceh, bel, beh, oddsel, m2t, mstr = _l1attn_consts()
    ident = np.eye(128, dtype=np.float32)
    Wq, Wk, Wv = inp['L1_attn_Wq'], inp['L1_attn_Wk'], inp['L1_attn_Wv']
    Wb, Wa, Wg = inp['L1_attn_Wb'], inp['L1_attn_Wa'], inp['L1_attn_Wg']
    cqw, ckw, cvw = inp['L1_attn_conv_q_w'], inp['L1_attn_conv_k_w'], inp['L1_attn_conv_v_w']
    in_maps = []
    for c in range(8):
        b, par = c // 2, c % 2
        hsl = slice(par * 512, (par + 1) * 512)
        Wall = np.zeros((2048, L1_NPC), np.float32)
        Wall[:, 0:512] = Wq[:, hsl]
        Wall[:, 512:1536] = np.concatenate(
            [Wk[:, j * 1024 + par * 512:j * 1024 + (par + 1) * 512] for j in range(2)], 1)
        Wall[:, 1536:2560] = np.concatenate(
            [Wv[:, j * 1024 + par * 512:j * 1024 + (par + 1) * 512] for j in range(2)], 1)
        Wall[:, 2560:3072] = Wg[:, hsl]
        Wall[:, 3072:3080] = np.concatenate(
            [Wb[:, j * 8 + par * 4:j * 8 + (par + 1) * 4] for j in range(2)], 1)
        Wall[:, 3080:3084] = Wa[:, par * 4:(par + 1) * 4]
        cwm = np.concatenate(
            [cqw[hsl, 0, :]] +
            [ckw[j * 1024 + par * 512:j * 1024 + (par + 1) * 512, 0, :] for j in range(2)] +
            [cvw[j * 1024 + par * 512:j * 1024 + (par + 1) * 512, 0, :] for j in range(2)], 0)
        in_maps.append({
            "x2b": np.ascontiguousarray(x2[b]),
            "Wall": Wall, "cw": np.ascontiguousarray(cwm),
            "alog": np.ascontiguousarray(np.broadcast_to(
                inp['L1_attn_A_log'][par * 4:(par + 1) * 4][None, :], (128, 4))),
            "dtb": np.ascontiguousarray(np.broadcast_to(
                inp['L1_attn_dt_bias'][par * 4:(par + 1) * 4][None, :], (128, 4))),
            "nwm": np.ascontiguousarray(np.broadcast_to(
                inp['L1_attn_norm_w'][None, :], (128, 128))),
            "cel": cel, "ceh": ceh, "bel": bel, "beh": beh, "oddsel": oddsel,
            "m2t": m2t, "mstr": mstr, "identt": ident,
        })
    res = run_bass_kernel_spmd(nc, in_maps, core_ids=list(range(8)))
    og = np.zeros((B, L, 1024), ml_dtypes.bfloat16)
    for c in range(8):
        b, par = c // 2, c % 2
        og[b, :, par * 512:(par + 1) * 512] = res.results[c]["og"]
    return og


def run_l1ffn(x2, og, inp):
    nc = _get("l1ffn", build_l1ffn)
    wqgT = np.ascontiguousarray(inp['L1_ffn_wq_gate'].T)
    wquT = np.ascontiguousarray(inp['L1_ffn_wq_up'].T)
    wqdT = np.ascontiguousarray(inp['L1_ffn_wq_down'].T)
    sgm = np.ascontiguousarray(np.broadcast_to(inp['L1_ffn_s_gate'][None, :], (128, QH)))
    summ = np.ascontiguousarray(np.broadcast_to(inp['L1_ffn_s_up'][None, :], (128, QH)))
    sdm = np.ascontiguousarray(np.broadcast_to(inp['L1_ffn_s_down'][None, :], (128, D)))
    in_maps = []
    for c in range(8):
        b, par = c // 2, c % 2
        sl = slice(par * CTOK, (par + 1) * CTOK)
        in_maps.append({
            "x2o": np.ascontiguousarray(x2[b, sl]),
            "ogf": np.ascontiguousarray(og[b, sl]),
            "Wo1": inp['L1_attn_Wo'],
            "wqgT": wqgT, "wquT": wquT, "wqdT": wqdT,
            "sgm": sgm, "sum_": summ, "sdm": sdm,
        })
    res = run_bass_kernel_spmd(nc, in_maps, core_ids=list(range(8)))
    out = np.zeros((B, L, D), np.float32)
    for c in range(8):
        b, par = c // 2, c % 2
        out[b, par * CTOK:(par + 1) * CTOK] = res.results[c]["x4"]
    return out


def kernel(**inputs):
    inp = inputs
    x = np.ascontiguousarray(inp['L0_attn_x'], dtype=np.float32)
    attn_half = run_l0a_heads(x, inp['L0_attn_Wqk'], inp['L0_attn_bqk'],
                              inp['L0_attn_Wv'], inp['L0_attn_bv'])
    x1 = run_l0a_epi(x, attn_half, inp['L0_attn_Wu'], inp['L0_attn_bu'],
                     inp['L0_attn_Wo'], inp['L0_attn_bo'],
                     inp['L0_attn_ln_g'], inp['L0_attn_ln_b'])
    x2 = run_l0ffn(x1, inp)
    og = run_l1attn(x2, inp)
    x4 = run_l1ffn(x2, og, inp)
    return x4
